# version 03
# baseline (speedup 1.0000x reference)
import numpy as np
import ml_dtypes
from contextlib import ExitStack

import concourse.bass as bass
import concourse.mybir as mybir
from concourse.bass_utils import run_bass_kernel_spmd

F32 = mybir.dt.float32
BF16 = mybir.dt.bfloat16
AF = mybir.ActivationFunctionType
ALU = mybir.AluOpType
AX = mybir.AxisListType
BF16NP = ml_dtypes.bfloat16

NCORES = 8
D = 1024
NH = 16
DH = 64
TB = 8192
TC = 2048
HC = 1024
NCH = 8
EXPB = np.float32(-20.0)
SC8 = np.float32(0.125)

fp = np.float32


def chunk_of(core):
    return core, 7 - core


def shard_rows(x):
    out = []
    for core in range(NCORES):
        c0, c1 = chunk_of(core)
        out.append(np.concatenate([x[0, c0 * HC:(c0 + 1) * HC], x[1, c1 * HC:(c1 + 1) * HC]], axis=0))
    return out


def unshard_rows(shards):
    x = np.empty((2, TB) + shards[0].shape[1:], dtype=shards[0].dtype)
    for core in range(NCORES):
        c0, c1 = chunk_of(core)
        x[0, c0 * HC:(c0 + 1) * HC] = shards[core][:HC]
        x[1, c1 * HC:(c1 + 1) * HC] = shards[core][HC:]
    return x


def fmT(a):
    return np.ascontiguousarray(np.asarray(a).T)


ENGS = ("sync", "scalar", "vector", "gpsimd", "tensor")
SEMS = ("dsy", "dsc", "dve", "dgp", "t", "v", "s", "g")
DMA_SEM = {"sync": "dsy", "scalar": "dsc", "vector": "dve", "gpsimd": "dgp"}


class Prog:
    def __init__(self, name):
        self.nc = bass.Bass("TRN2", target_bir_lowering=False, name=name)
        self.tasks = {e: [] for e in ENGS}
        self.cnt = {sname: 0 for sname in SEMS}
        self.wm = {e: {} for e in ENGS}

    def n(self, s):
        return self.cnt[s]

    def at(self, *sems):
        return [(s, self.cnt[s]) for s in sems]

    def _resolve(self, eng, waits):
        res = []
        for w in waits:
            name, val = w if isinstance(w, tuple) else (w, self.cnt[w])
            if val <= 0:
                continue
            if self.wm[eng].get(name, -1) < val:
                res.append((name, val))
                self.wm[eng][name] = val
        return res

    AUTOSYNC = {"vector": "v", "scalar": "s", "gpsimd": "g"}

    def op(self, eng, builder, waits=(), inc=None, inc_n=1, dma_op=False):
        auto = self.AUTOSYNC.get(eng)
        if auto is not None:
            waits = list(waits) + [(auto, self.cnt[auto])]
            if inc is None and not dma_op:
                inc = auto
        res = self._resolve(eng, waits)
        tgt = None
        if inc is not None:
            self.cnt[inc] += inc_n
            tgt = (inc, inc_n)

        def task(engine, _res=res, _b=builder, _tgt=tgt):
            for name, val in _res:
                engine.wait_ge(self._sem[name], val)
            inst = _b(engine)
            if _tgt is not None and inst is not None:
                inst.then_inc(self._sem[_tgt[0]], _tgt[1])
        self.tasks[eng].append(task)

    def dma(self, eng, out, in_, waits=(), accum=False, slow=False):
        kw = {"accum_op": ALU.add} if accum else {}
        if slow:
            kw["allow_slow_non_contiguous"] = True
        self.op(eng, lambda E, o=out, i=in_, k=kw: E.dma_start(out=o, in_=i, **k),
                waits=waits, inc=DMA_SEM[eng], inc_n=16, dma_op=True)

    def mm(self, out, lhsT, rhs, start, stop, waits=(), inc=None, tile_position=None):
        self.op("tensor",
                lambda E, o=out, l=lhsT, r=rhs, s=start, e=stop, tp=tile_position:
                E.matmul(o, l, r, start=s, stop=e, tile_position=tp),
                waits=waits, inc=inc)

    def trans(self, out, in_, ident, waits=(), inc=None):
        self.op("tensor", lambda E, o=out, i=in_, d=ident: E.transpose(o, i, d),
                waits=waits, inc=inc)

    def act(self, out, in_, func, bias=0.0, scale=1.0, waits=(), inc=None, accum_out=None):
        bias = fp(bias) if isinstance(bias, float) else bias
        scale = fp(scale) if isinstance(scale, float) else scale
        if accum_out is None:
            self.op("scalar", lambda E, o=out, i=in_, f=func, b=bias, sc=scale:
                    E.activation(o, i, f, bias=b, scale=sc), waits=waits, inc=inc)
        else:
            self.op("scalar", lambda E, o=out, i=in_, f=func, b=bias, sc=scale, a=accum_out:
                    E.activation(o, i, f, bias=b, scale=sc, accum_out=a), waits=waits, inc=inc)

    def tt(self, eng, out, a, b, aop, waits=(), inc=None):
        self.op(eng, lambda E, o=out, x=a, y=b, p=aop: E.tensor_tensor(o, x, y, p),
                waits=waits, inc=inc)

    def ts(self, eng, out, in_, s1, s2, op0, op1=ALU.bypass, waits=(), inc=None):
        s1 = float(s1) if isinstance(s1, (float, np.floating)) else s1
        s2 = float(s2) if isinstance(s2, (float, np.floating)) else s2
        self.op(eng, lambda E, o=out, i=in_, a=s1, b=s2, p0=op0, p1=op1:
                E.tensor_scalar(o, i, a, b, p0, p1), waits=waits, inc=inc)

    def stt(self, eng, out, in0, scalar, in1, op0, op1, waits=(), inc=None):
        scalar = float(scalar) if isinstance(scalar, (float, np.floating)) else scalar
        self.op(eng, lambda E, o=out, i=in0, sc=scalar, j=in1, p0=op0, p1=op1:
                E.scalar_tensor_tensor(o, i, sc, j, p0, p1), waits=waits, inc=inc)

    def copy(self, eng, out, in_, waits=(), inc=None):
        self.op(eng, lambda E, o=out, i=in_: E.tensor_copy(o, i), waits=waits, inc=inc)

    def red(self, out, in_, aop, negate=False, waits=(), inc=None):
        self.op("vector", lambda E, o=out, i=in_, p=aop, ng=negate:
                E.tensor_reduce(o, i, AX.X, p, negate=ng), waits=waits, inc=inc)

    def recip(self, out, in_, waits=(), inc=None):
        self.op("vector", lambda E, o=out, i=in_: E.reciprocal(o, i), waits=waits, inc=inc)

    def memset(self, eng, ap_, val, waits=(), inc=None):
        self.op(eng, lambda E, a=ap_, v=val: E.memset(a, v), waits=waits, inc=inc)

    def nop(self, eng, waits=(), inc=None):
        self.op(eng, lambda E: None, waits=waits, inc=inc, dma_op=True)

    def finalize(self):
        nc = self.nc
        with ExitStack() as st:
            self._sem = {name: st.enter_context(nc.semaphore(name)) for name in SEMS}
            block = st.enter_context(nc.Block())
            for eng in ENGS:
                if not self.tasks[eng]:
                    continue
                tasks = self.tasks[eng]

                def runner(engine, _tasks=tasks):
                    for t in _tasks:
                        t(engine)
                getattr(block, eng)(runner)
        return nc



class LNBufs:
    def __init__(self, nc, tag, psum=None):
        self.ones_c = nc.alloc_sbuf_tensor(f"{tag}_ones_c", [128, 1], F32)
        self.ones_r = nc.alloc_sbuf_tensor(f"{tag}_ones_r", [1, 128], F32)
        self.sq = nc.alloc_sbuf_tensor(f"{tag}_sq", [128, 2 * 512], F32)
        self.stat = nc.alloc_sbuf_tensor(f"{tag}_stat", [1, 6 * 512], F32)
        self.xf = nc.alloc_sbuf_tensor(f"{tag}_xf", [128, 512], F32)
        if psum is None:
            self.ps_sum = nc.alloc_psum_tensor(f"{tag}_ps_sum", [1, 512], F32)
            self.ps_sq = nc.alloc_psum_tensor(f"{tag}_ps_sq", [1, 512], F32)
            self.ps_a = nc.alloc_psum_tensor(f"{tag}_ps_a", [128, 512], F32)
            self.ps_b = nc.alloc_psum_tensor(f"{tag}_ps_b", [128, 512], F32)
        else:
            self.ps_sum, self.ps_sq, self.ps_a, self.ps_b = psum

    def init(self, p):
        p.memset("vector", self.ones_c[:, :], 1.0, inc="v")
        p.memset("vector", self.ones_r[:, :], 1.0, inc="v")


def emit_ln_strip(p, B, x_of_chunk, ts, xn_cb, x_ready=(), T=TC):
    strip_gate = p.at("v", "s")
    for c in range(NCH):
        p.mm(B.ps_sum[:, :], B.ones_c[:, :], x_of_chunk(c), c == 0, c == NCH - 1,
             waits=(list(x_ready) + strip_gate) if c == 0 else (),
             inc="t" if c == NCH - 1 else None)
    for c in range(NCH):
        slot = c % 2
        sq = B.sq[:, slot * 512:(slot + 1) * 512]
        p.act(sq, x_of_chunk(c), AF.Square,
              waits=list(x_ready) + strip_gate + ([("t", p.n("t"))] if c >= 2 else []), inc="s")
        p.mm(B.ps_sq[:, :], B.ones_c[:, :], sq, c == 0, c == NCH - 1,
             waits=[("s", p.n("s"))], inc="t")
    mm_done = p.at("t")
    mean = B.stat[:, 0:512]; ex2 = B.stat[:, 512:1024]; var = B.stat[:, 1024:1536]
    rstd = B.stat[:, 1536:2048]; brow = B.stat[:, 2048:2560]; sqv = B.stat[:, 2560:3072]
    p.ts("vector", mean, B.ps_sum[:, :], 1.0 / D, None, ALU.mult, waits=mm_done)
    p.ts("vector", ex2, B.ps_sq[:, :], 1.0 / D, None, ALU.mult)
    p.tt("vector", var, mean, mean, ALU.mult)
    p.tt("vector", var, ex2, var, ALU.subtract, inc="v")
    p.act(sqv, var, AF.Sqrt, bias=1e-5, waits=[("v", p.n("v"))], inc="s")
    p.recip(rstd, sqv, waits=[("s", p.n("s"))])
    p.tt("vector", brow, mean, rstd, ALU.mult)
    p.ts("vector", brow, brow, -1.0, None, ALU.mult, inc="v")
    stats_v = p.at("v")
    p.mm(B.ps_a[:, :], B.ones_r[:, :], rstd, True, True, waits=stats_v)
    p.mm(B.ps_b[:, :], B.ones_r[:, :], brow, True, True, inc="t")
    bc = p.at("t")
    for c in range(NCH):
        p.tt("vector", B.xf[:, :], x_of_chunk(c), B.ps_a[:, :], ALU.mult, waits=bc)
        p.tt("vector", B.xf[:, :], B.xf[:, :], B.ps_b[:, :], ALU.add, inc="v")
        xn_cb(c, B.xf[:, :], p.at("v"))



def build_A():
    p = Prog("mkA")
    nc = p.nc
    x_in = nc.dram_tensor("x", [D, TC], F32, kind="ExternalInput").ap()
    wdkvT_in = nc.dram_tensor("wdkvT", [D, 128], F32, kind="ExternalInput").ap()
    wg_in = nc.dram_tensor("wg", [D, 8], F32, kind="ExternalInput").ap()
    bg_in = nc.dram_tensor("bg", [1, 8], F32, kind="ExternalInput").ap()
    xn_out = nc.dram_tensor("xn0", [D, TC], BF16, kind="ExternalOutput").ap()
    ckv_out = nc.dram_tensor("ckv", [128, TC], BF16, kind="ExternalOutput").ap()
    mw_out = nc.dram_tensor("mw", [TC, 8], F32, kind="ExternalOutput").ap()

    x_sb = nc.alloc_sbuf_tensor("x_sb", [128, NCH * TC], F32)
    xn_sb = nc.alloc_sbuf_tensor("xn_sb", [128, NCH * TC], BF16)
    wdkvT_sb = nc.alloc_sbuf_tensor("wdkvT_sb", [128, NCH * 128], F32)
    wdkvT_bf = nc.alloc_sbuf_tensor("wdkvT_bf", [128, NCH * 128], BF16)
    wg_sb = nc.alloc_sbuf_tensor("wg_sb", [128, NCH * 8], F32)
    bg_sb = nc.alloc_sbuf_tensor("bg_sb", [1, 8], F32)
    ckv_sc = nc.alloc_sbuf_tensor("ckv_sc", [128, TC], BF16)
    xfs = nc.alloc_sbuf_tensor("xfs", [128, NCH * 512], F32)
    gm = nc.alloc_sbuf_tensor("gm", [128, 4], F32)
    gmn = nc.alloc_sbuf_tensor("gmn", [128, 4], F32)
    gsc = nc.alloc_sbuf_tensor("gsc", [128, 4 * 8], F32)
    ge = nc.alloc_sbuf_tensor("ge", [128, 4 * 8], F32)
    gs = nc.alloc_sbuf_tensor("gs", [128, 4], F32)
    gw = nc.alloc_sbuf_tensor("gw", [128, 4], F32)
    mw_sb = nc.alloc_sbuf_tensor("mw_sb", [128, 16 * 8], F32)

    B = LNBufs(nc, "ln")
    ps_g = nc.alloc_psum_tensor("ps_g", [128, 4 * 8], F32)
    ps_ckv = nc.alloc_psum_tensor("ps_ckv", [128, 512], F32)

    p.dma("sync", x_sb[:, :].rearrange("p (c t) -> p c t", c=NCH), x_in.rearrange("(c p) t -> p c t", p=128))
    p.dma("sync", wdkvT_sb[:, :].rearrange("p (c l) -> p c l", c=NCH), wdkvT_in.rearrange("(c p) l -> p c l", p=128))
    p.dma("sync", wg_sb[:, :].rearrange("p (c e) -> p c e", c=NCH), wg_in.rearrange("(c p) e -> p c e", p=128))
    p.dma("sync", bg_sb[:, :], bg_in)
    lded = p.at("dsy")

    B.init(p)
    p.copy("vector", wdkvT_bf[:, :], wdkvT_sb[:, :], waits=lded, inc="v")
    prep = p.at("v")

    for ts in range(TC // 512):
        x_of_chunk = lambda c, _ts=ts: x_sb[:, c * TC + _ts * 512: c * TC + (_ts + 1) * 512]

        def xn_cb(c, xf, waits, _ts=ts):
            dst_f = xfs[:, c * 512:(c + 1) * 512]
            dst_b = xn_sb[:, c * TC + _ts * 512: c * TC + (_ts + 1) * 512]
            p.copy("vector", dst_f, xf, waits=p.at("t"), inc="v")
            p.copy("vector", dst_b, xf, inc="v")
            p.dma("gpsimd", xn_out[c * 128:(c + 1) * 128, _ts * 512:(_ts + 1) * 512],
                  dst_b, waits=[("v", p.n("v"))])

        emit_ln_strip(p, B, x_of_chunk, ts, xn_cb, x_ready=lded)
        xn_v = p.at("v")

        for tb in range(4):
            pg = ps_g[:, tb * 8:(tb + 1) * 8]
            for c in range(NCH):
                xf = xfs[:, c * 512 + tb * 128: c * 512 + (tb + 1) * 128]
                wgc = wg_sb[:, c * 8:(c + 1) * 8]
                p.mm(pg, xf, wgc, c == 0, False,
                     waits=(xn_v + p.at("v", "s")) if c == 0 else ())
            p.mm(pg, B.ones_r[:1, :], bg_sb[:, :], False, True, waits=lded, inc="t")
        gmm = p.at("t")
        for tb in range(4):
            pg = ps_g[:, tb * 8:(tb + 1) * 8]
            m_ = gm[:, tb:tb + 1]; mn_ = gmn[:, tb:tb + 1]
            sg = gsc[:, tb * 8:(tb + 1) * 8]
            ee = ge[:, tb * 8:(tb + 1) * 8]
            ss = gs[:, tb:tb + 1]; ww = gw[:, tb:tb + 1]
            mwt = mw_sb[:, (ts * 4 + tb) * 8:(ts * 4 + tb + 1) * 8]
            p.red(m_, pg, ALU.max, waits=gmm, inc="v")
            p.ts("vector", mn_, m_, -1.0, None, ALU.mult, inc="v")
            p.act(sg, pg, AF.Sign, bias=m_, scale=-1.0, waits=[("v", p.n("v"))], inc="s")
            p.act(ee, pg, AF.Exp, bias=mn_, scale=1.0, inc="s")
            p.red(ss, ee, ALU.add, negate=True, waits=[("s", p.n("s"))])
            p.recip(ww, ss)
            p.ts("vector", mwt, sg, 1.0, ww, ALU.subtract, ALU.mult, inc="v")
        for c in range(NCH):
            xb = xn_sb[:, c * TC + ts * 512: c * TC + (ts + 1) * 512]
            wc = wdkvT_bf[:, c * 128:(c + 1) * 128]
            p.mm(ps_ckv[:, :], wc, xb, c == 0, c == NCH - 1,
                 waits=(prep + xn_v + p.at("v")) if c == 0 else (),
                 inc="t" if c == NCH - 1 else None)
        p.copy("vector", ckv_sc[:, ts * 512:(ts + 1) * 512], ps_ckv[:, :],
               waits=p.at("t"), inc="v")
        p.dma("gpsimd", ckv_out[:, ts * 512:(ts + 1) * 512],
              ckv_sc[:, ts * 512:(ts + 1) * 512], waits=[("v", p.n("v"))])

    p.dma("gpsimd", mw_out.rearrange("(tb p) e -> p tb e", p=128),
          mw_sb[:, :].rearrange("p (tb e) -> p tb e", e=8), waits=p.at("v"))
    p.nop("gpsimd", waits=p.at("dgp"))
    p.nop("sync", waits=p.at("dgp"))
    return p


def host_A(inputs, xsh_fm):
    wdkvT = fmT(inputs["L0_attn_W_dkv"]).astype(np.float32)
    wg = np.ascontiguousarray(inputs["L0_ffn_Wg"]).astype(np.float32)
    bg = np.ascontiguousarray(inputs["L0_ffn_bg"]).reshape(1, 8).astype(np.float32)
    return [{"x": xsh_fm[c], "wdkvT": wdkvT, "wg": wg, "bg": bg} for c in range(NCORES)]


NKB = 64
NST = 4
NEGF = -1.0e9


def strip_list(core):
    c0, c1 = chunk_of(core)
    return [(0, c0 * HC), (0, c0 * HC + 512), (1, c1 * HC), (1, c1 * HC + 512)]



def build_B():
    p = Prog("mkB")
    nc = p.nc
    x_in = nc.dram_tensor("x", [D, TC], F32, kind="ExternalInput").ap()
    xn0o_in = nc.dram_tensor("xn0o", [D, TC], BF16, kind="ExternalInput").ap()
    ckvf_in = nc.dram_tensor("ckvf", [NST * 128, NKB * 128], BF16, kind="ExternalInput").ap()
    ckvt_in = nc.dram_tensor("ckvt", [NST * NKB * 128, 128], BF16, kind="ExternalInput").ap()
    vm_in = nc.dram_tensor("vm", [128, NST * NKB], BF16, kind="ExternalInput").ap()
    dmask_in = nc.dram_tensor("dmask", [128, 4 * 512], F32, kind="ExternalInput").ap()
    wdq_in = nc.dram_tensor("wdq", [128, D], F32, kind="ExternalInput").ap()
    wdqT_in = nc.dram_tensor("wdqT", [D, 128], F32, kind="ExternalInput").ap()
    wuqT_in = nc.dram_tensor("wuqT", [128, D], F32, kind="ExternalInput").ap()
    wuqr_in = nc.dram_tensor("wuqr", [64, NH * 128], F32, kind="ExternalInput").ap()
    wuk_in = nc.dram_tensor("wuk", [D, 128], F32, kind="ExternalInput").ap()
    wuv_in = nc.dram_tensor("wuv", [D, 128], F32, kind="ExternalInput").ap()
    woT_in = nc.dram_tensor("woT", [D, D], F32, kind="ExternalInput").ap()
    xng_in = nc.dram_tensor("xng", [D, 2 * TB], BF16, kind="ExternalInput").ap()
    mwe_in = nc.dram_tensor("mwe", [1, 2 * TB], F32, kind="ExternalInput").ap()
    w1e_in = nc.dram_tensor("w1e", [D, 4096], F32, kind="ExternalInput").ap()
    w2e_in = nc.dram_tensor("w2e", [4096, D], F32, kind="ExternalInput").ap()
    b1e_in = nc.dram_tensor("b1e", [128, 32], F32, kind="ExternalInput").ap()
    b2e_in = nc.dram_tensor("b2e", [128, 8], F32, kind="ExternalInput").ap()
    x1a_out = nc.dram_tensor("x1a", [D, TC], F32, kind="ExternalOutput").ap()
    ye_out = nc.dram_tensor("ye", [D, 2 * TB], F32, kind="ExternalOutput").ap()

    ql_s = nc.alloc_sbuf_tensor("ql_s", [128, NH * 512], BF16)
    ckv_s = nc.alloc_sbuf_tensor("ckv_s", [128, 2 * NKB * 128], BF16)
    x1a_s = nc.alloc_sbuf_tensor("x1a_s", [128, NCH * 512], F32)
    smal = nc.alloc_sbuf_tensor("smal", [128, 15360], BF16)
    s1t = nc.alloc_sbuf_tensor("s1t", [128, 8192], BF16)
    stage = nc.alloc_sbuf_tensor("stage", [128, 4096], F32)
    hx = nc.alloc_sbuf_tensor("hx", [128, 16384], BF16)
    fsml = nc.alloc_sbuf_tensor("fsml", [128, 3072], F32)
    vm_s = nc.alloc_sbuf_tensor("vm_s", [128, NST * NKB], BF16)
    probs = nc.alloc_sbuf_tensor("probs", [128, 2 * 512], BF16)
    yev = nc.alloc_sbuf_tensor("yev", [128, 2 * 512], F32)
    row1 = nc.alloc_sbuf_tensor("row1", [1, 2048], F32)
    ones_r = nc.alloc_sbuf_tensor("ones_r", [1, 128], BF16)
    ones_rf = nc.alloc_sbuf_tensor("ones_rf", [1, 128], F32)

    wdqT_bf = smal[:, 0:1024]; wuk_bf = smal[:, 1024:2048]; wuv_bf = smal[:, 2048:3072]
    wuqT_bf = smal[:, 3072:4096]; wdq_bf = smal[:, 4096:5120]; v0_bf = smal[:, 5120:6144]
    qa_bf = smal[:, 6144:8192]; m_bf = smal[:, 8192:10240]
    wuqr_bf = smal[0:64, 10240:12288]; p64 = smal[0:64, 12288:14336]; pstage = smal[:, 14336:14464]

    dmask_s = fsml[:, 0:2048]; bc_s = fsml[:, 2048:2560]
    b1_s = fsml[:, 2560:2592]; b2_s = fsml[:, 2592:2600]
    invs = row1[:1, 0:512]
    mwrow = row1[:1, 1024:1536]

    ps_a = nc.alloc_psum_tensor("ps_a", [128, 512], F32)
    ps_b = nc.alloc_psum_tensor("ps_b", [128, 512], F32)
    ps_ctx = nc.alloc_psum_tensor("ps_ctx", [128, 512], F32)
    ps_sum = nc.alloc_psum_tensor("ps_sum", [1, 512], F32)
    ps_bc = nc.alloc_psum_tensor("ps_bc", [128, 512], F32)
    ps_y = nc.alloc_psum_tensor("ps_y", [128, 512], F32)
    ps_h0 = nc.alloc_psum_tensor("ps_h0", [128, 512], F32)
    ps_h1 = nc.alloc_psum_tensor("ps_h1", [128, 512], F32)
    ps_sc = [ps_a, ps_b]

    p.memset("vector", ones_r[:, :], 1.0)
    p.memset("vector", ones_rf[:, :], 1.0)
    p.dma("sync", vm_s[:, :], vm_in)
    p.dma("sync", dmask_s[:, :], dmask_in)
    p.dma("sync", hx[:, :].rearrange("p (c t) -> p c t", c=NCH), xn0o_in.rearrange("(c p) t -> p c t", p=128))

    def ld1(dst, src2d, rows=128):
        w = src2d.shape[-1] if src2d.ndim == 2 else src2d.shape[-1] * src2d.shape[-2]
        p.dma("sync", stage[0:rows, 0:w] if src2d.ndim == 2 else
              stage[0:rows, 0:w].rearrange("p (c l) -> p c l", c=src2d.shape[1]),
              src2d, waits=p.at("v"))
        p.copy("vector", dst, stage[0:rows, 0:w], waits=p.at("dsy"))

    ld1(wdqT_bf, wdqT_in.rearrange("(c p) l -> p c l", p=128))
    ld1(wuk_bf, wuk_in.rearrange("(c p) l -> p c l", p=128))
    ld1(wuv_bf, wuv_in.rearrange("(c p) l -> p c l", p=128))
    ld1(wuqT_bf, wuqT_in)
    ld1(wdq_bf, wdq_in)
    ld1(wuqr_bf, wuqr_in, rows=64)
    casts_done = p.at("v")

    for bc in range(8):
        for n in range(2):
            p.mm(ps_sc[n][:, :], wuqT_bf[:, bc * 128:(bc + 1) * 128], wdq_bf[:, n * 512:(n + 1) * 512],
                 True, True, waits=list(casts_done) + [("v", p.n("v"))], inc="t")
            p.copy("vector", s1t[:, bc * 1024 + n * 512: bc * 1024 + (n + 1) * 512], ps_sc[n][:, :],
                   waits=p.at("t"))
    for ac in range(8):
        for bc in range(8):
            p.mm(ps_ctx[:, :128], s1t[:, bc * 1024 + ac * 128: bc * 1024 + (ac + 1) * 128],
                 wuk_bf[:, bc * 128:(bc + 1) * 128], bc == 0, bc == 7,
                 waits=p.at("v") if bc == 0 else (), inc="t" if bc == 7 else None)
        p.copy("vector", pstage[:, 0:128], ps_ctx[:, :128], waits=p.at("t", "dgp"))
        for hh in range(2):
            h = 2 * ac + hh
            p.dma("gpsimd", p64[0:64, h * 128:(h + 1) * 128],
                  pstage[hh * 64:(hh + 1) * 64, 0:128], waits=p.at("v"))
    mready = p.at("dgp", "v")
    for h in range(NH):
        p.mm(ps_bc[:, :128], wuqr_bf[0:64, h * 128:(h + 1) * 128], p64[0:64, h * 128:(h + 1) * 128],
             True, True, waits=list(mready) + [("v", p.n("v"))], inc="t")
        p.copy("vector", m_bf[:, h * 128:(h + 1) * 128], ps_bc[:, :128], waits=p.at("t"))
    for n in range(2):
        p.dma("sync", stage[:, :].rearrange("p (c d) -> p c d", c=NCH),
              woT_in.rearrange("(c p) d -> p c d", p=128)[:, :, n * 512:(n + 1) * 512],
              waits=p.at("v"))
        p.copy("vector", ql_s[:, 0:4096], stage[:, :], waits=p.at("dsy"))
        for cc in range(8):
            p.mm(ps_ctx[:, :], wuv_bf[:, cc * 128:(cc + 1) * 128], ql_s[:, cc * 512:(cc + 1) * 512],
                 cc == 0, cc == 7, waits=p.at("v") if cc == 0 else (), inc="t" if cc == 7 else None)
        p.copy("vector", v0_bf[:, n * 512:(n + 1) * 512], ps_ctx[:, :], waits=p.at("t"))
    for ts in range(NST):
        for c in range(NCH):
            p.mm(ps_ctx[:, :], wdqT_bf[:, c * 128:(c + 1) * 128],
                 hx[:, c * TC + ts * 512: c * TC + (ts + 1) * 512], c == 0, c == NCH - 1,
                 waits=p.at("v", "dsy") if c == 0 else (), inc="t" if c == NCH - 1 else None)
        p.copy("vector", qa_bf[:, ts * 512:(ts + 1) * 512], ps_ctx[:, :], waits=p.at("t"))
    p0_done = p.at("v")

    for ts in range(NST):
        p.dma("sync", ckv_s[:, 0:NKB * 128], ckvf_in[ts * 128:(ts + 1) * 128, :],
              waits=p.at("t", "s") if ts > 0 else p0_done)
        p.dma("sync", ckv_s[:, NKB * 128:2 * NKB * 128].rearrange("p (kb l) -> p kb l", l=128),
              ckvt_in[ts * NKB * 128:(ts + 1) * NKB * 128, :].rearrange("(kb p) l -> p kb l", p=128))
        for h in range(NH):
            p.mm(ps_bc[:, :], m_bf[:, h * 128:(h + 1) * 128], qa_bf[:, ts * 512:(ts + 1) * 512],
                 True, True, waits=p.at("v") + (p0_done if (ts == 0 and h == 0) else []), inc="t")
            p.copy("vector", ql_s[:, h * 512:(h + 1) * 512], ps_bc[:, :], waits=p.at("t"))
        p.dma("gpsimd", x1a_s[:, :].rearrange("p (c t) -> p c t", t=512),
              x_in[:, ts * 512:(ts + 1) * 512].rearrange("(c p) t -> p c t", p=128),
              waits=p.at("v", "dgp"))
        kv_ld = p.at("dsy")
        ql_v = p.at("v")
        xi_ld = p.at("dgp")
        for h in range(NH):
            qlh = ql_s[:, h * 512:(h + 1) * 512]
            score_t = {}
            for kb in range(2):
                p.mm(ps_sc[kb][:, :], ckv_s[:, kb * 128:(kb + 1) * 128], qlh, True, True,
                     waits=list(kv_ld) + list(ql_v) + [("s", p.n("s"))], inc="t")
                score_t[kb] = p.n("t")
            hd_v = p.at("v")
            for kb in range(NKB):
                slot = kb % 2
                if kb < 4:
                    p.tt("vector", ps_sc[slot][:, :], ps_sc[slot][:, :],
                         dmask_s[:, kb * 512:(kb + 1) * 512], ALU.add,
                         waits=[("t", score_t[kb])])
                    ewait = [("v", p.n("v"))]
                else:
                    ewait = [("t", score_t[kb])]
                p.act(probs[:, slot * 512:(slot + 1) * 512], ps_sc[slot][:, :], AF.Exp,
                      bias=float(EXPB), scale=float(SC8), waits=ewait)
                ecnt = p.n("s")
                p.mm(ps_ctx[:, :], ckv_s[:, NKB * 128 + kb * 128: NKB * 128 + (kb + 1) * 128],
                     probs[:, slot * 512:(slot + 1) * 512], kb == 0, kb == NKB - 1,
                     waits=[("s", ecnt)] + (hd_v if kb == 0 else []))
                p.mm(ps_sum[:, :], vm_s[:, ts * NKB + kb: ts * NKB + kb + 1],
                     probs[:, slot * 512:(slot + 1) * 512], kb == 0, kb == NKB - 1,
                     waits=(hd_v if kb == 0 else ()), inc="t")
                if kb + 2 < NKB:
                    p.mm(ps_sc[slot][:, :], ckv_s[:, (kb + 2) * 128:(kb + 3) * 128], qlh, True, True,
                         waits=[("s", ecnt)], inc="t")
                    score_t[kb + 2] = p.n("t")
            p.recip(invs, ps_sum[:, :], waits=p.at("t"))
            p.mm(ps_bc[:, :], ones_rf[:, :], invs, True, True, waits=p.at("v"), inc="t")
            p.copy("vector", bc_s[:, :], ps_bc[:, :], waits=p.at("t"))
            p.tt("vector", probs[:, 0:512], ps_ctx[:, :], bc_s[:, :], ALU.mult)
            p.mm(ps_y[0:64, :], v0_bf[:, h * 64:(h + 1) * 64], probs[:, 0:512], True, True,
                 waits=p.at("v") + list(xi_ld), inc="t")
            dst = x1a_s[(h % 2) * 64:(h % 2) * 64 + 64, (h // 2) * 512:(h // 2) * 512 + 512]
            p.tt("vector", dst, dst, ps_y[0:64, :], ALU.add, waits=p.at("t"))
        p.dma("gpsimd", x1a_out.rearrange("(c p) t -> p c t", p=128)[:, :, ts * 512:(ts + 1) * 512],
              x1a_s[:, :].rearrange("p (c t) -> p c t", t=512), waits=p.at("v"))

    p1_done = p.at("t", "v", "s")

    def w1tile(dk, ht):
        col = dk * 2048 + ht * 128
        return ql_s[:, col:col + 128] if col < 8192 else s1t[:, col - 8192:col - 8192 + 128]

    ps_o = [ps_a, ps_b, ps_ctx, ps_bc]
    for half in range(2):
        for dk in range(8):
            p.dma("sync", stage[:, (dk % 2) * 2048:(dk % 2) * 2048 + 2048],
                  w1e_in[dk * 128:(dk + 1) * 128, half * 2048:(half + 1) * 2048],
                  waits=[("v", p.n("v"))])
            base = dk * 2048
            dst = ql_s[:, base:base + 2048] if base < 8192 else s1t[:, base - 8192:base - 8192 + 2048]
            p.copy("vector", dst, stage[:, (dk % 2) * 2048:(dk % 2) * 2048 + 2048],
                   waits=p.at("dsy") + (p1_done if half == 0 else []))
        for hc in range(16):
            p.dma("sync", stage[:, (hc % 2) * 1024:(hc % 2) * 1024 + 1024],
                  w2e_in[(half * 16 + hc) * 128:(half * 16 + hc + 1) * 128, :],
                  waits=[("v", p.n("v"))])
            p.copy("vector", ckv_s[:, hc * 1024:(hc + 1) * 1024],
                   stage[:, (hc % 2) * 1024:(hc % 2) * 1024 + 1024],
                   waits=p.at("dsy") + (p1_done if half == 0 else []))
        if half == 0:
            p.dma("sync", stage[:, 0:32], b1e_in, waits=p.at("v"))
            p.dma("sync", stage[:, 32:40], b2e_in)
            p.copy("vector", b1_s, stage[:, 0:32], waits=p.at("dsy"))
            p.copy("vector", b2_s, stage[:, 32:40])
        wcast = p.at("v")

        for tsg in range(32):
            xslot = tsg % 2
            xbuf = hx[:, 8192 + xslot * 4096: 8192 + (xslot + 1) * 4096]
            p.dma("gpsimd", xbuf.rearrange("p (c t) -> p c t", t=512),
                  xng_in[:, tsg * 512:(tsg + 1) * 512].rearrange("(c p) t -> p c t", p=128),
                  waits=[("t", p.n("t"))])
            p.dma("gpsimd", mwrow, mwe_in[:, tsg * 512:(tsg + 1) * 512])
            xg_ld = p.at("dgp")
            p.mm(ps_y[:, :], ones_rf[:, :], mwrow, True, True,
                 waits=list(xg_ld) + [("v", p.n("v"))], inc="t")
            p.copy("vector", bc_s[:, :], ps_y[:, :], waits=p.at("t"))
            mwbc_v = p.at("v")
            for ht in range(16):
                psh = [ps_h0, ps_h1][ht % 2]
                for dk in range(8):
                    p.mm(psh[:, :], w1tile(dk, ht), xbuf[:, dk * 512:(dk + 1) * 512],
                         dk == 0, dk == 7,
                         waits=(list(xg_ld) + list(wcast) + [("s", p.n("s"))]) if dk == 0 else (),
                         inc="t" if dk == 7 else None)
                p.act(hx[:, ht * 512:(ht + 1) * 512], psh[:, :], AF.Silu,
                      bias=b1_s[:, half * 16 + ht: half * 16 + ht + 1],
                      waits=p.at("t"))
            h_done = p.at("s")
            for dc in range(8):
                pso = ps_o[dc % 4]
                for ht in range(16):
                    p.mm(pso[:, :], ckv_s[:, ht * 1024 + dc * 128: ht * 1024 + dc * 128 + 128],
                         hx[:, ht * 512:(ht + 1) * 512], ht == 0, ht == 15,
                         waits=(list(h_done) + [("v", p.n("v"))]) if ht == 0 else (),
                         inc="t" if ht == 15 else None)
                ysl = yev[:, (dc % 2) * 512:(dc % 2) * 512 + 512]
                if half == 0:
                    p.tt("vector", ysl, pso[:, :], bc_s[:, :], ALU.mult,
                         waits=p.at("t") + mwbc_v + [("dgp", p.n("dgp"))])
                else:
                    p.stt("vector", ysl, pso[:, :], b2_s[:, dc:dc + 1], bc_s[:, :],
                          ALU.add, ALU.mult, waits=p.at("t") + mwbc_v + [("dgp", p.n("dgp"))])
                p.dma("gpsimd", ye_out[dc * 128:(dc + 1) * 128, tsg * 512:(tsg + 1) * 512], ysl,
                      waits=[("v", p.n("v"))], accum=(half == 1))

    p.nop("gpsimd", waits=p.at("dgp"))
    p.nop("sync", waits=p.at("dgp", "dsy"))
    return p



NEGC = -1.0e9


def make_dmask():
    m = np.zeros((128, 4 * 512), dtype=np.float32)
    for kb in range(4):
        kk = np.arange(128)[:, None]
        qq = np.arange(512)[None, :]
        m[:, kb * 512:(kb + 1) * 512] = np.where(kb * 128 + kk > qq, NEGC, 0.0)
    return m


def full_from_shards(shards_fm):
    Dp = shards_fm[0].shape[0]
    full = np.empty((Dp, 2 * TB), dtype=shards_fm[0].dtype)
    for core in range(NCORES):
        c0, c1 = chunk_of(core)
        full[:, c0 * HC:(c0 + 1) * HC] = shards_fm[core][:, 0:HC]
        full[:, TB + c1 * HC:TB + (c1 + 1) * HC] = shards_fm[core][:, HC:TC]
    return full


def rotated_kv(full_fm_b, core, width):
    W = width
    fm = np.zeros((NST * W, NKB * 128), dtype=full_fm_b[0].dtype)
    tm = np.zeros((NST * NKB * 128, W), dtype=full_fm_b[0].dtype)
    for s, (b, pos0) in enumerate(strip_list(core)):
        src = full_fm_b[b]
        fm[s * W:(s + 1) * W, 0:512] = src[:, pos0:pos0 + 512]
        fm[s * W:(s + 1) * W, 512:512 + pos0] = src[:, 0:pos0]
        tm[s * NKB * 128: s * NKB * 128 + 512, :] = src[:, pos0:pos0 + 512].T
        tm[s * NKB * 128 + 512: s * NKB * 128 + 512 + pos0, :] = src[:, 0:pos0].T
    return fm, tm


def make_vm(core):
    vm = np.zeros((128, NST * NKB), dtype=BF16NP)
    for s, (b, pos0) in enumerate(strip_list(core)):
        nval = 4 + pos0 // 128
        vm[:, s * NKB: s * NKB + nval] = 1.0
    return vm


def host_B(inputs, xsh_fm, xn0_sh, ckv_sh, mw_sh):
    tmp = full_from_shards(ckv_sh)
    ckv_full = [tmp[:, 0:TB], tmp[:, TB:2 * TB]]
    xng = full_from_shards(xn0_sh)
    mw_full = unshard_rows(mw_sh).reshape(2 * TB, 8)
    dmask = make_dmask()
    wdq = np.ascontiguousarray(inputs["L0_attn_W_dq"]).astype(np.float32)
    wdqT = fmT(wdq).astype(np.float32)
    wuq = np.asarray(inputs["L0_attn_W_uq"]).astype(np.float32)
    wuqT = fmT(wuq).astype(np.float32)
    wuqr = np.ascontiguousarray(np.concatenate(
        [wuq[64 * h:64 * (h + 1), :] for h in range(NH)], axis=1)).astype(np.float32)
    wuk = np.ascontiguousarray(inputs["L0_attn_W_uk"]).astype(np.float32)
    wuv = np.ascontiguousarray(inputs["L0_attn_W_uv"]).astype(np.float32)
    woT = fmT(inputs["L0_attn_W_o"]).astype(np.float32)
    W1 = np.asarray(inputs["L0_ffn_W1"]); b1 = np.asarray(inputs["L0_ffn_b1"])
    W2 = np.asarray(inputs["L0_ffn_W2"]); b2 = np.asarray(inputs["L0_ffn_b2"])
    maps = []
    for core in range(NCORES):
        ckvf, ckvt = rotated_kv(ckv_full, core, 128)
        e = core
        maps.append({
            "x": xsh_fm[core], "xn0o": xn0_sh[core],
            "ckvf": ckvf, "ckvt": ckvt, "vm": make_vm(core), "dmask": dmask,
            "wdq": wdq, "wdqT": wdqT, "wuqT": wuqT,
            "wuqr": wuqr, "wuk": wuk, "wuv": wuv, "woT": woT,
            "xng": xng, "mwe": np.ascontiguousarray(mw_full[:, e]).reshape(1, 2 * TB).astype(np.float32),
            "w1e": np.ascontiguousarray(W1[e]).astype(np.float32),
            "w2e": np.ascontiguousarray(W2[e]).astype(np.float32),
            "b1e": np.ascontiguousarray(b1[e].reshape(32, 128).T).astype(np.float32),
            "b2e": np.ascontiguousarray(b2[e].reshape(8, 128).T).astype(np.float32),
        })
    return maps



def build_C():
    p = Prog("mkC")
    nc = p.nc
    x1a_in = nc.dram_tensor("x1a", [D, TC], F32, kind="ExternalInput").ap()
    ye_ins = [nc.dram_tensor(f"ye{r}", [D, TC], F32, kind="ExternalInput").ap() for r in range(NCORES)]
    wq_in = nc.dram_tensor("wqTp", [D, D], F32, kind="ExternalInput").ap()
    wk_in = nc.dram_tensor("wkTp", [D, D], F32, kind="ExternalInput").ap()
    wv_in = nc.dram_tensor("wvT", [D, D], F32, kind="ExternalInput").ap()
    sin_in = nc.dram_tensor("sint", [64, TC], F32, kind="ExternalInput").ap()
    cos_in = nc.dram_tensor("cost", [64, TC], F32, kind="ExternalInput").ap()
    x1_out = nc.dram_tensor("x1", [D, TC], F32, kind="ExternalOutput").ap()
    xn1_out = nc.dram_tensor("xn1", [D, TC], BF16, kind="ExternalOutput").ap()
    q1_out = nc.dram_tensor("q1", [64, NH * TC], BF16, kind="ExternalOutput").ap()
    k1_out = nc.dram_tensor("k1", [64, NH * TC], BF16, kind="ExternalOutput").ap()
    vext_out = nc.dram_tensor("vext", [TC, NH * 65], BF16, kind="ExternalOutput").ap()
    gmp_out = nc.dram_tensor("gmp", [128, 16], F32, kind="ExternalOutput").ap()

    x_sb = nc.alloc_sbuf_tensor("x_sb", [128, NCH * TC], F32)
    xn_sb = nc.alloc_sbuf_tensor("xn_sb", [128, NCH * TC], BF16)
    wq_bf = nc.alloc_sbuf_tensor("wq_bf", [128, NCH * D], BF16)
    wk_bf = nc.alloc_sbuf_tensor("wk_bf", [128, NCH * D], BF16)
    wv_bf = nc.alloc_sbuf_tensor("wv_bf", [128, NCH * D], BF16)
    stage = nc.alloc_sbuf_tensor("stage", [128, 4096], F32)
    sin_sb = nc.alloc_sbuf_tensor("sin_sb", [64, TC], F32)
    cos_sb = nc.alloc_sbuf_tensor("cos_sb", [64, TC], F32)
    scr1 = nc.alloc_sbuf_tensor("scr1", [64, 512], F32)
    scr2 = nc.alloc_sbuf_tensor("scr2", [64, 512], F32)
    ob1 = nc.alloc_sbuf_tensor("ob1", [64, 512], BF16)
    ob2 = nc.alloc_sbuf_tensor("ob2", [64, 512], BF16)
    vstage = nc.alloc_sbuf_tensor("vstage", [128, NH * 65], BF16)
    gm_acc = nc.alloc_sbuf_tensor("gm_acc", [128, 16], F32)
    gm_scr = nc.alloc_sbuf_tensor("gm_scr", [128, 1], F32)
    B = LNBufs(nc, "ln")
    ps_q = nc.alloc_psum_tensor("ps_q", [128, 512], F32)
    ps_v0 = nc.alloc_psum_tensor("ps_v0", [128, 512], F32)
    ps_v1 = nc.alloc_psum_tensor("ps_v1", [128, 512], F32)

    B.init(p)
    p.dma("sync", x_sb[:, :].rearrange("p (c t) -> p c t", c=NCH), x1a_in.rearrange("(c p) t -> p c t", p=128))
    xld = p.at("dsy")
    for r in range(NCORES):
        p.dma("gpsimd", x_sb[:, :].rearrange("p (c t) -> p c t", c=NCH),
              ye_ins[r].rearrange("(c p) t -> p c t", p=128),
              waits=list(xld) + [("dgp", p.n("dgp"))], accum=True)
    p.dma("scalar", sin_sb[:, :], sin_in)
    p.dma("scalar", cos_sb[:, :], cos_in)
    xsum = p.at("dsy", "dgp")
    for w_in, w_bf in ((wq_in, wq_bf), (wk_in, wk_bf), (wv_in, wv_bf)):
        for half in range(2):
            p.dma("scalar", stage[:, :].rearrange("p (c e) -> p c e", c=NCH),
                  w_in.rearrange("(c p) e -> p c e", p=128)[:, :, half * 512:(half + 1) * 512],
                  waits=[("v", p.n("v"))])
            for c in range(NCH):
                p.copy("vector", w_bf[:, c * D + half * 512: c * D + (half + 1) * 512],
                       stage[:, c * 512:(c + 1) * 512], waits=p.at("dsc"))
    wv_done = p.at("v")
    p.memset("vector", gm_acc[:, :], 0.0)
    for h in range(NH):
        p.memset("vector", vstage[:, h * 65 + 64: h * 65 + 65], 1.0)
    p.dma("gpsimd", x1_out.rearrange("(c p) t -> p c t", p=128),
          x_sb[:, :].rearrange("p (c t) -> p c t", c=NCH), waits=xsum)

    for ts in range(NST):
        x_of_chunk = lambda c, _ts=ts: x_sb[:, c * TC + _ts * 512: c * TC + (_ts + 1) * 512]
        bhalf = 0 if ts < 2 else 1

        def xn_cb(c, xf, waits, _ts=ts, _b=bhalf):
            dst_b = xn_sb[:, c * TC + _ts * 512: c * TC + (_ts + 1) * 512]
            p.copy("vector", dst_b, xf)
            p.red(gm_scr[:, :], xf, ALU.add)
            acc = gm_acc[:, _b * 8 + c: _b * 8 + c + 1]
            p.tt("vector", acc, acc, gm_scr[:, :], ALU.add)
            p.dma("gpsimd", xn1_out[c * 128:(c + 1) * 128, _ts * 512:(_ts + 1) * 512],
                  dst_b, waits=[("v", p.n("v"))])

        emit_ln_strip(p, B, x_of_chunk, ts, xn_cb, x_ready=xsum)
        xn_v = p.at("v")

        sslice = sin_sb[:, ts * 512:(ts + 1) * 512]
        cslice = cos_sb[:, ts * 512:(ts + 1) * 512]
        for w_bf, out_ap in ((wq_bf, q1_out), (wk_bf, k1_out)):
            for c in range(NCH):
                for dk in range(NCH):
                    p.mm(ps_q[:, :], w_bf[:, dk * D + c * 128:(dk * D) + (c + 1) * 128],
                         xn_sb[:, dk * TC + ts * 512: dk * TC + (ts + 1) * 512],
                         dk == 0, dk == NCH - 1,
                         waits=(list(wv_done) + list(xn_v) + [("v", p.n("v")), ("dgp", p.n("dgp"))]) if dk == 0 else (),
                         inc="t" if dk == NCH - 1 else None)
                mmw = p.at("t")
                x1b = ps_q[0:64, :]
                x2b = ps_q[64:128, :]
                p.tt("vector", scr1[:, :], x1b, cslice, ALU.mult, waits=list(mmw) + p.at("dsc"))
                p.tt("vector", scr2[:, :], x2b, sslice, ALU.mult)
                p.tt("vector", ob1[:, :], scr1[:, :], scr2[:, :], ALU.subtract)
                p.tt("vector", scr1[:, :], x1b, sslice, ALU.mult)
                p.tt("vector", scr2[:, :], x2b, cslice, ALU.mult)
                p.tt("vector", ob2[:, :], scr1[:, :], scr2[:, :], ALU.add)
                obv = p.at("v")
                hA, hB = 2 * c, 2 * c + 1
                p.dma("gpsimd", out_ap[0:32, hA * TC + ts * 512: hA * TC + (ts + 1) * 512],
                      ob1[0:32, :], waits=obv)
                p.dma("gpsimd", out_ap[32:64, hA * TC + ts * 512: hA * TC + (ts + 1) * 512],
                      ob2[0:32, :])
                p.dma("gpsimd", out_ap[0:32, hB * TC + ts * 512: hB * TC + (ts + 1) * 512],
                      ob1[32:64, :])
                p.dma("gpsimd", out_ap[32:64, hB * TC + ts * 512: hB * TC + (ts + 1) * 512],
                      ob2[32:64, :])
        for tb in range(4):
            tcol = ts * 512 + tb * 128
            for half, psv in ((0, ps_v0), (1, ps_v1)):
                for dk in range(NCH):
                    p.mm(psv[:, :], xn_sb[:, dk * TC + tcol: dk * TC + tcol + 128],
                         wv_bf[:, dk * D + half * 512:(dk * D) + half * 512 + 512],
                         dk == 0, dk == NCH - 1,
                         waits=(list(xn_v) + [("v", p.n("v")), ("dgp", p.n("dgp"))]) if dk == 0 else (),
                         inc="t" if dk == NCH - 1 else None)
            vmw = p.at("t")
            for h in range(NH):
                psv = ps_v0 if h < 8 else ps_v1
                p.copy("vector", vstage[:, h * 65: h * 65 + 64],
                       psv[:, (h % 8) * 64:(h % 8) * 64 + 64], waits=vmw)
            p.dma("gpsimd", vext_out[tcol:tcol + 128, :], vstage[:, :], waits=[("v", p.n("v"))])

    p.dma("gpsimd", gmp_out, gm_acc[:, :], waits=p.at("v"))
    p.nop("gpsimd", waits=p.at("dgp"))
    p.nop("sync", waits=p.at("dgp", "dsy"))
    return p


def rope_tables(core):
    inv = (10000.0 ** (-(np.arange(32, dtype=np.float64) * 2.0 / 64.0)))
    pos = np.empty(TC, dtype=np.float64)
    for s, (b, pos0) in enumerate(strip_list(core)):
        pos[s * 512:(s + 1) * 512] = np.arange(pos0, pos0 + 512)
    fr = pos[None, :] * inv[:, None]
    fr = np.concatenate([fr, fr], axis=0)
    return (np.sin(fr.astype(np.float32).astype(np.float64)).astype(np.float32),
            np.cos(fr.astype(np.float32).astype(np.float64)).astype(np.float32))


def rope_perm():
    perm = []
    for c in range(NCH):
        A, Bh = 2 * c, 2 * c + 1
        perm += list(64 * A + np.arange(0, 64, 2)) + list(64 * Bh + np.arange(0, 64, 2))
        perm += list(64 * A + np.arange(1, 64, 2)) + list(64 * Bh + np.arange(1, 64, 2))
    return np.asarray(perm)


def host_C(inputs, x1a_sh, ye_all):
    perm = rope_perm()
    wqT = fmT(inputs["L1_attn_wq"]).astype(np.float32)[:, perm]
    wkT = fmT(inputs["L1_attn_wk"]).astype(np.float32)[:, perm]
    wvT = fmT(inputs["L1_attn_wv"]).astype(np.float32)
    maps = []
    for core in range(NCORES):
        c0, c1 = chunk_of(core)
        m = {"x1a": x1a_sh[core], "wqTp": np.ascontiguousarray(wqT),
             "wkTp": np.ascontiguousarray(wkT), "wvT": wvT}
        sin_t, cos_t = rope_tables(core)
        m["sint"] = sin_t; m["cost"] = cos_t
        for r in range(NCORES):
            ye = ye_all[r]
            sl = np.empty((D, TC), dtype=np.float32)
            sl[:, 0:HC] = ye[:, c0 * HC:(c0 + 1) * HC]
            sl[:, HC:TC] = ye[:, TB + c1 * HC:TB + (c1 + 1) * HC]
            m[f"ye{r}"] = sl
        maps.append(m)
    return maps



TWIST_MAP = [(4, 1.0), (5, 1.0), (6, -1.0), (7, -1.0), (0, -1.0), (1, -1.0), (2, 1.0), (3, 1.0)]


def build_D():
    p = Prog("mkD")
    nc = p.nc
    x1_in = nc.dram_tensor("x1", [D, TC], F32, kind="ExternalInput").ap()
    xn1_in = nc.dram_tensor("xn1", [D, TC], BF16, kind="ExternalInput").ap()
    q1_in = nc.dram_tensor("q1", [64, NH * TC], BF16, kind="ExternalInput").ap()
    k1r_in = nc.dram_tensor("k1r", [NST * 64, NH * NKB * 128], BF16, kind="ExternalInput").ap()
    v1r_in = nc.dram_tensor("v1r", [NST * NKB * 128, NH * 65], BF16, kind="ExternalInput").ap()
    dmask_in = nc.dram_tensor("dmask", [128, 4 * 512], BF16, kind="ExternalInput").ap()
    gmp_in = nc.dram_tensor("gmp", [128, 16 * NCORES], F32, kind="ExternalInput").ap()
    ar_in = nc.dram_tensor("ar", [1, 1], F32, kind="ExternalInput").ap()
    wo1T_in = nc.dram_tensor("wo1T", [D, D], F32, kind="ExternalInput").ap()
    cw1_in = nc.dram_tensor("cw1", [D, D], F32, kind="ExternalInput").ap()
    cw2_in = nc.dram_tensor("cw2", [D, 512], F32, kind="ExternalInput").ap()
    cw3_in = nc.dram_tensor("cw3", [512, 256], F32, kind="ExternalInput").ap()
    cw4_in = nc.dram_tensor("cw4", [256, 1], F32, kind="ExternalInput").ap()
    cb1_in = nc.dram_tensor("cb1", [128, 8], F32, kind="ExternalInput").ap()
    cb2_in = nc.dram_tensor("cb2", [128, 4], F32, kind="ExternalInput").ap()
    cb3_in = nc.dram_tensor("cb3", [128, 2], F32, kind="ExternalInput").ap()
    cb4_in = nc.dram_tensor("cb4", [1, 1], F32, kind="ExternalInput").ap()
    gw1_in = nc.dram_tensor("gw1", [D, 512], F32, kind="ExternalInput").ap()
    gw2_in = nc.dram_tensor("gw2", [512, 256], F32, kind="ExternalInput").ap()
    gw3_in = nc.dram_tensor("gw3", [256, 1], F32, kind="ExternalInput").ap()
    gb1_in = nc.dram_tensor("gb1", [128, 4], F32, kind="ExternalInput").ap()
    gb2_in = nc.dram_tensor("gb2", [128, 2], F32, kind="ExternalInput").ap()
    gb3_in = nc.dram_tensor("gb3", [1, 1], F32, kind="ExternalInput").ap()
    wq2T_in = nc.dram_tensor("wq2T", [D, D], F32, kind="ExternalInput").ap()
    wk2T_in = nc.dram_tensor("wk2T", [D, D], F32, kind="ExternalInput").ap()
    wv2T_in = nc.dram_tensor("wv2T", [D, D], F32, kind="ExternalInput").ap()
    bq2_in = nc.dram_tensor("bq2", [128, 8], F32, kind="ExternalInput").ap()
    bk2_in = nc.dram_tensor("bk2", [128, 8], F32, kind="ExternalInput").ap()
    bv2_in = nc.dram_tensor("bv2", [1, D], F32, kind="ExternalInput").ap()

    x2_out = nc.dram_tensor("x2", [D, TC], F32, kind="ExternalOutput").ap()
    xn2_out = nc.dram_tensor("xn2", [D, TC], BF16, kind="ExternalOutput").ap()
    q2_out = nc.dram_tensor("q2", [64, NH * TC], BF16, kind="ExternalOutput").ap()
    k2_out = nc.dram_tensor("k2", [64, NH * TC], BF16, kind="ExternalOutput").ap()
    v2e_out = nc.dram_tensor("v2e", [TC, NH * 65], BF16, kind="ExternalOutput").ap()
    attn_scr = nc.dram_tensor("attn_scr", [D, TC], F32).ap()

    kbuf = nc.alloc_sbuf_tensor("kbuf", [128, 16384], BF16)
    vbuf = nc.alloc_sbuf_tensor("vbuf", [128, 8 * 1040], BF16)
    wbig = nc.alloc_sbuf_tensor("wbig", [128, 8192], BF16)
    wo1_bf = nc.alloc_sbuf_tensor("wo1_bf", [128, 8192], BF16)
    q1s = nc.alloc_sbuf_tensor("q1s", [128, 8192], BF16)
    acc = nc.alloc_sbuf_tensor("acc", [128, 8192], F32)
    a1s = nc.alloc_sbuf_tensor("a1s", [128, 4096], F32)
    ctxbf = nc.alloc_sbuf_tensor("ctxbf", [128, 4096], BF16)
    probs = nc.alloc_sbuf_tensor("probs", [128, 1024], BF16)
    dmask_s = nc.alloc_sbuf_tensor("dmask_s", [128, 2048], BF16)
    fsml = nc.alloc_sbuf_tensor("fsml", [128, 4096], F32)
    stage = nc.alloc_sbuf_tensor("stage", [128, 2048], F32)
    vstage = nc.alloc_sbuf_tensor("vstage", [128, NH * 65], BF16)
    ones_r = nc.alloc_sbuf_tensor("ones_r", [1, 128], BF16)
    ones_rf = nc.alloc_sbuf_tensor("ones_rf", [1, 128], F32)

    bc_s = fsml[:, 0:512]
    gm = fsml[:, 512:528]
    biases = fsml[:, 528:592]
    invs = fsml[:1, 592:1104]
    bv2row = fsml[:1, 1104:2128]
    gfv = fsml[:1, 2128:2130]; Ab = fsml[:1, 2130:2132]
    scoef = fsml[:1, 2132:2133]; t1r = fsml[:1, 2134:2136]
    tf = fsml[:1, 2560:3072]; crow = fsml[:1, 3072:3584]; crn = fsml[:1, 3584:4096]
    cb1_s = biases[:, 0:8]; cb2_s = biases[:, 8:12]; cb3_s = biases[:, 12:14]; cb4_s = biases[:1, 14:15]
    gb1_s = biases[:, 16:20]; gb2_s = biases[:, 20:22]; gb3_s = biases[:1, 22:23]
    bq2_s = biases[:, 24:32]; bk2_s = biases[:, 32:40]; ar_s = biases[:1, 40:41]

    ps_a = nc.alloc_psum_tensor("ps_a", [128, 512], F32)
    ps_b = nc.alloc_psum_tensor("ps_b", [128, 512], F32)
    ps_pv = nc.alloc_psum_tensor("ps_pv", [128, 512], F32)
    ps_bc = nc.alloc_psum_tensor("ps_bc", [128, 512], F32)
    ps_w = nc.alloc_psum_tensor("ps_w", [128, 512], F32)
    ps_w2 = nc.alloc_psum_tensor("ps_w2", [128, 512], F32)
    ps_r1 = nc.alloc_psum_tensor("ps_r1", [1, 512], F32)
    ps_sc = [ps_a, ps_b]

    p.memset("vector", ones_r[:, :], 1.0)
    p.memset("vector", ones_rf[:, :], 1.0)
    p.dma("sync", dmask_s[:, :], dmask_in)
    src3 = wo1T_in.rearrange("(c p) e -> p c e", p=128)
    for c0 in range(0, NCH, 2):
        n = min(2, NCH - c0)
        p.dma("scalar", stage[:, 0:n * D].rearrange("p (c e) -> p c e", e=D),
              src3[:, c0:c0 + n, :], waits=[("v", p.n("v"))])
        p.copy("vector", wo1_bf[:, c0 * D:(c0 + n) * D], stage[:, 0:n * D], waits=p.at("dsc"))
    wo1_v = p.at("v")

    for ts in range(NST):
        p.dma("scalar", q1s[0:64, :].rearrange("q (h t) -> q h t", t=512),
              q1_in.rearrange("q (h t) -> q h t", t=TC)[:, :, ts * 512:(ts + 1) * 512],
              waits=[("t", p.n("t"))] + (list(wo1_v) if ts == 0 else []))
        p.memset("vector", acc[:, :], 0.0, waits=p.at("t"))
        qld = p.at("dsc")
        NGRP = NKB // 8
        for grp in range(NGRP):
            for octt in range(2):
                p.dma("sync", kbuf[0:64, octt * 8192:(octt + 1) * 8192].rearrange(
                          "q (h t) -> q h t", t=1024),
                      k1r_in[ts * 64:(ts + 1) * 64, :].rearrange("q (h t) -> q h t", t=NKB * 128)
                          [:, octt * 8:(octt + 1) * 8, grp * 1024:(grp + 1) * 1024],
                      waits=[("t", p.n("t")), ("s", p.n("s"))])
            p.dma("sync", vbuf[:, :].rearrange("p (kb c) -> p kb c", c=1040),
                  v1r_in[(ts * NKB + grp * 8) * 128:(ts * NKB + (grp + 1) * 8) * 128, :]
                      .rearrange("(kb p) c -> p kb c", p=128))
            gld = p.at("dsy")
            for h in range(NH):
                kbase = (h // 8) * 8192 + (h % 8) * 1024
                qh = q1s[0:64, h * 512:(h + 1) * 512]
                score_t = {}
                hd_v = p.at("v")
                for kb in range(2):
                    p.mm(ps_sc[kb][:, :], kbuf[0:64, kbase + kb * 128: kbase + (kb + 1) * 128],
                         qh, True, True, waits=list(gld) + list(qld) + [("s", p.n("s"))], inc="t")
                    score_t[kb] = p.n("t")
                for kb in range(8):
                    sl2 = kb % 2
                    if grp == 0 and kb < 4:
                        p.tt("vector", ps_sc[sl2][:, :], ps_sc[sl2][:, :],
                             dmask_s[:, kb * 512:(kb + 1) * 512], ALU.add,
                             waits=[("t", score_t[kb])])
                        ewait = [("v", p.n("v"))]
                    else:
                        ewait = [("t", score_t[kb])]
                    p.act(probs[:, sl2 * 512:(sl2 + 1) * 512], ps_sc[sl2][:, :], AF.Exp,
                          bias=float(EXPB), scale=float(SC8), waits=ewait)
                    ecnt = p.n("s")
                    p.mm(ps_pv[0:65, :], vbuf[:, kb * 1040 + h * 65: kb * 1040 + h * 65 + 65],
                         probs[:, sl2 * 512:(sl2 + 1) * 512], kb == 0, kb == 7,
                         waits=[("s", ecnt)] + (hd_v if kb == 0 else []),
                         inc="t" if kb == 7 else None)
                    if kb + 2 < 8:
                        p.mm(ps_sc[sl2][:, :], kbuf[0:64, kbase + (kb + 2) * 128: kbase + (kb + 3) * 128],
                             qh, True, True, waits=[("s", ecnt)], inc="t")
                        score_t[kb + 2] = p.n("t")
                asl = acc[0:65, h * 512:(h + 1) * 512]
                p.tt("vector", asl, asl, ps_pv[0:65, :], ALU.add, waits=p.at("t"))
        for h in range(NH):
            p.recip(invs, acc[64:65, h * 512:(h + 1) * 512])
            p.mm(ps_bc[:, :], ones_rf[:, :], invs, True, True, waits=p.at("v"), inc="t")
            p.copy("vector", bc_s, ps_bc[:, :], waits=p.at("t"))
            p.tt("vector", ctxbf[(h % 2) * 64:(h % 2) * 64 + 64, (h // 2) * 512:(h // 2) * 512 + 512],
                 acc[0:64, h * 512:(h + 1) * 512], bc_s[0:64, :], ALU.mult)
        ctx_v = p.at("v")
        for oc in range(NCH):
            for dk in range(NCH):
                p.mm(ps_w[:, :], wo1_bf[:, dk * D + oc * 128: dk * D + (oc + 1) * 128],
                     ctxbf[:, dk * 512:(dk + 1) * 512], dk == 0, dk == NCH - 1,
                     waits=(list(wo1_v) + list(ctx_v) + [("v", p.n("v")), ("dgp", p.n("dgp"))]) if dk == 0 else (),
                     inc="t" if dk == NCH - 1 else None)
            p.copy("vector", a1s[:, oc * 512:(oc + 1) * 512], ps_w[:, :], waits=p.at("t"))
        p.dma("gpsimd", attn_scr.rearrange("(c p) t -> p c t", p=128)[:, :, ts * 512:(ts + 1) * 512],
              a1s[:, :].rearrange("p (c t) -> p c t", t=512), waits=p.at("v"))

    pass1_done = p.at("t", "v", "s", "dgp")

    gmb = ctxbf[:, 0:16]
    h1g = ctxbf[:, 16:20]; h2g = ctxbf[:, 20:22]
    p.dma("sync", stage[:, 0:128], gmp_in, waits=pass1_done)
    p.copy("vector", gm, stage[:, 0:16], waits=p.at("dsy"))
    for r in range(1, NCORES):
        p.tt("vector", gm, gm, stage[:, r * 16:(r + 1) * 16], ALU.add)
    p.ts("vector", gmb, gm, 1.0 / TB, None, ALU.mult)

    cw1_bf = kbuf[:, 0:8192]
    cw2_bf = kbuf[:, 8192:12288]
    cw3_bf = kbuf[:, 12288:13312]
    cw4_bf = kbuf[:, 13312:13320]
    gw1_bf = kbuf[:, 0:4096]
    gw2_bf = kbuf[:, 4096:5120]
    gw3_bf = kbuf[:, 5120:5128]
    wq2_bf = wbig[:, 0:8192]
    wk2_bf = wo1_bf[:, 0:8192]
    wv2_bf = vbuf[:, 0:8192]
    bv2_bf = probs[:1, 0:1024]

    def ldw(dst, src2d, extra=()):
        src3 = src2d.rearrange("(c p) e -> p c e", p=128)
        C, E = src3.shape[1], src3.shape[2]
        cpg = max(1, 2048 // E)
        for c0 in range(0, C, cpg):
            n = min(cpg, C - c0)
            p.dma("scalar", stage[:, 0:n * E].rearrange("p (c e) -> p c e", e=E),
                  src3[:, c0:c0 + n, :], waits=[("v", p.n("v"))], slow=(E == 1))
            p.copy("vector", dst[:, c0 * E:(c0 + n) * E], stage[:, 0:n * E],
                   waits=p.at("dsc") + list(extra))

    ldw(gw1_bf, gw1_in)
    ldw(gw2_bf, gw2_in)
    ldw(gw3_bf, gw3_in)
    p.dma("scalar", stage[:, 0:8], cb1_in, waits=[("v", p.n("v"))])
    p.dma("scalar", stage[:, 8:12], cb2_in)
    p.dma("scalar", stage[:, 12:14], cb3_in)
    p.dma("scalar", stage[:1, 14:15], cb4_in[:1, :])
    p.dma("scalar", stage[:, 16:20], gb1_in)
    p.dma("scalar", stage[:, 20:22], gb2_in)
    p.dma("scalar", stage[:1, 22:23], gb3_in[:1, :])
    p.dma("scalar", stage[:, 24:32], bq2_in)
    p.dma("scalar", stage[:, 32:40], bk2_in)
    p.dma("scalar", stage[:1, 40:41], ar_in[:1, :])
    p.dma("scalar", stage[:1, 48:1072], bv2_in)
    p.copy("vector", biases[:, 0:48], stage[:, 0:48], waits=p.at("dsc"))
    p.copy("vector", bv2row, stage[:1, 48:1072])
    p.copy("vector", bv2_bf, bv2row)

    for b in range(2):
        for oc in range(4):
            for dk in range(8):
                p.mm(ps_w[:, 0:1], gw1_bf[:, dk * 512 + oc * 128: dk * 512 + (oc + 1) * 128],
                     gmb[:, b * 8 + dk: b * 8 + dk + 1], dk == 0, dk == 7,
                     waits=[("v", p.n("v")), ("s", p.n("s"))] if dk == 0 else (),
                     inc="t" if dk == 7 else None)
            p.act(h1g[:, oc:oc + 1], ps_w[:, 0:1], AF.Gelu, bias=gb1_s[:, oc:oc + 1], waits=p.at("t"))
        for oc in range(2):
            for dk in range(4):
                p.mm(ps_w[:, 0:1], gw2_bf[:, dk * 256 + oc * 128: dk * 256 + (oc + 1) * 128],
                     h1g[:, dk:dk + 1], dk == 0, dk == 3,
                     waits=[("s", p.n("s")), ("v", p.n("v"))] if dk == 0 else (), inc="t" if dk == 3 else None)
            p.act(h2g[:, oc:oc + 1], ps_w[:, 0:1], AF.Gelu, bias=gb2_s[:, oc:oc + 1], waits=p.at("t"))
        for dk in range(2):
            p.mm(ps_r1[:1, 0:1], gw3_bf[:, dk:dk + 1], h2g[:, dk:dk + 1],
                 dk == 0, dk == 1, waits=[("s", p.n("s"))] if dk == 0 else (), inc="t" if dk == 1 else None)
        p.act(gfv[:, b:b + 1], ps_r1[:1, 0:1], AF.Sigmoid, bias=gb3_s, waits=p.at("t"))
    gfb = p.at("t", "s")
    ldw(cw1_bf, cw1_in, extra=gfb)
    ldw(cw2_bf, cw2_in)
    ldw(cw3_bf, cw3_in)
    ldw(cw4_bf, cw4_in)
    ldw(wq2_bf, wq2T_in)
    ldw(wk2_bf, wk2T_in)
    ldw(wv2_bf, wv2T_in)
    wld = p.at("v")
    p.ts("vector", t1r, gfv[:1, 0:2], 0.7, -0.5, ALU.mult, ALU.add, waits=p.at("s"))
    p.tt("vector", t1r, t1r, ar_s.broadcast_to([1, 2]), ALU.mult)
    p.ts("vector", Ab, t1r, 2.0, 0.1, ALU.mult, ALU.add)
    p.ts("vector", scoef, ar_s, 0.6, None, ALU.mult)
    gf_v = p.at("v")

    x2s = a1s
    outbf = q1s[:, 0:4096]
    h1b = q1s[:, 4096:8192]
    xn2s = q1s[:, 4096:8192]
    h2b = ctxbf[:, 0:2048]
    h3b = ctxbf[:, 2048:3072]
    cbp = stage[:, 0:512]
    cbm = stage[:, 512:1024]
    B = LNBufs(nc, "ln", psum=(ps_a[:1, :], ps_b[:1, :], ps_bc, ps_pv))
    B.init(p)
    for h in range(NH):
        p.memset("vector", vstage[:, h * 65 + 64: h * 65 + 65], 1.0)

    for ts in range(NST):
        bidx = 0 if ts < 2 else 1
        p.dma("sync", x2s[:, :].rearrange("p (c t) -> p c t", t=512),
              x1_in[:, ts * 512:(ts + 1) * 512].rearrange("(c p) t -> p c t", p=128),
              waits=[("v", p.n("v")), ("dgp", p.n("dgp")), ("t", p.n("t")), ("s", p.n("s"))])
        p.dma("gpsimd", x2s[:, :].rearrange("p (c t) -> p c t", t=512),
              attn_scr.rearrange("(c p) t -> p c t", p=128)[:, :, ts * 512:(ts + 1) * 512],
              waits=p.at("dsy"), accum=True)
        p.dma("scalar", outbf[:, :].rearrange("p (c t) -> p c t", t=512),
              xn1_in[:, ts * 512:(ts + 1) * 512].rearrange("(c p) t -> p c t", p=128),
              waits=[("t", p.n("t")), ("s", p.n("s")), ("v", p.n("v"))])
        xld = p.at("dsy", "dgp", "dsc")
        p.copy("vector", acc[:, 0:4096], outbf[:, :], waits=xld)
        pp = 0
        for cyc in range(3):
            src_o = acc[:, pp * 4096:(pp + 1) * 4096]
            dst_o = acc[:, (1 - pp) * 4096:(2 - pp) * 4096]
            p.copy("vector", outbf[:, :], src_o)
            ob_v = p.at("v")
            for oc in range(NCH):
                for dk in range(NCH):
                    p.mm(ps_w[:, :], cw1_bf[:, dk * 1024 + oc * 128: dk * 1024 + (oc + 1) * 128],
                         outbf[:, dk * 512:(dk + 1) * 512], dk == 0, dk == 7,
                         waits=(list(ob_v) + list(wld) + [("s", p.n("s"))]) if dk == 0 else (),
                         inc="t" if dk == 7 else None)
                p.act(h1b[:, oc * 512:(oc + 1) * 512], ps_w[:, :], AF.Gelu,
                      bias=cb1_s[:, oc:oc + 1], waits=p.at("t"))
            for oc in range(4):
                for dk in range(NCH):
                    p.mm(ps_w2[:, :], cw2_bf[:, dk * 512 + oc * 128: dk * 512 + (oc + 1) * 128],
                         h1b[:, dk * 512:(dk + 1) * 512], dk == 0, dk == 7,
                         waits=[("s", p.n("s"))] if dk == 0 else (), inc="t" if dk == 7 else None)
                p.act(h2b[:, oc * 512:(oc + 1) * 512], ps_w2[:, :], AF.Gelu,
                      bias=cb2_s[:, oc:oc + 1], waits=p.at("t"))
            for oc in range(2):
                for dk in range(4):
                    p.mm(ps_w[:, :], cw3_bf[:, dk * 256 + oc * 128: dk * 256 + (oc + 1) * 128],
                         h2b[:, dk * 512:(dk + 1) * 512], dk == 0, dk == 3,
                         waits=[("s", p.n("s")), ("v", p.n("v"))] if dk == 0 else (), inc="t" if dk == 3 else None)
                p.act(h3b[:, oc * 512:(oc + 1) * 512], ps_w[:, :], AF.Gelu,
                      bias=cb3_s[:, oc:oc + 1], waits=p.at("t"))
            for dk in range(2):
                p.mm(ps_r1[:1, :], cw4_bf[:, dk:dk + 1], h3b[:, dk * 512:(dk + 1) * 512],
                     dk == 0, dk == 1, waits=[("s", p.n("s"))] if dk == 0 else (), inc="t" if dk == 1 else None)
            p.act(tf, ps_r1[:1, :], AF.Sigmoid, bias=cb4_s, waits=p.at("t"))
            p.ts("vector", crow, tf, scoef[:1, 0:1], Ab[:1, bidx:bidx + 1], ALU.mult, ALU.add,
                 waits=gf_v + [("s", p.n("s"))])
            p.ts("vector", crn, crow, -1.0, None, ALU.mult)
            p.mm(ps_bc[:, :], ones_rf[:, :], crow, True, True, waits=p.at("v"), inc="t")
            p.copy("vector", cbp, ps_bc[:, :], waits=p.at("t"))
            p.mm(ps_bc[:, :], ones_rf[:, :], crn, True, True, waits=p.at("v"), inc="t")
            p.copy("vector", cbm, ps_bc[:, :], waits=p.at("t"))
            for c in range(NCH):
                srcc, sgn = TWIST_MAP[c]
                cb = cbp if sgn > 0 else cbm
                p.tt("vector", B.xf[:, :], src_o[:, srcc * 512:(srcc + 1) * 512], cb, ALU.mult)
                p.tt("vector", dst_o[:, c * 512:(c + 1) * 512], src_o[:, c * 512:(c + 1) * 512],
                     B.xf[:, :], ALU.add)
            pp = 1 - pp
        ffn_o = acc[:, pp * 4096:(pp + 1) * 4096]
        for c in range(NCH):
            sl3 = x2s[:, c * 512:(c + 1) * 512]
            p.tt("vector", sl3, sl3, ffn_o[:, c * 512:(c + 1) * 512], ALU.add)
        x2_v = p.at("v")
        p.dma("gpsimd", x2_out.rearrange("(c p) t -> p c t", p=128)[:, :, ts * 512:(ts + 1) * 512],
              x2s[:, :].rearrange("p (c t) -> p c t", t=512), waits=x2_v)
        x_of_chunk = lambda c, _ts=ts: x2s[:, c * 512:(c + 1) * 512]

        def xn_cb(c, xf, waits, _ts=ts):
            dst_b = xn2s[:, c * 512:(c + 1) * 512]
            p.copy("vector", dst_b, xf)
            p.dma("gpsimd", xn2_out[c * 128:(c + 1) * 128, _ts * 512:(_ts + 1) * 512],
                  dst_b, waits=[("v", p.n("v"))])

        emit_ln_strip(p, B, x_of_chunk, ts, xn_cb, x_ready=x2_v)
        xn_v = p.at("v")
        for w_bf, b_s, out_ap in ((wq2_bf, bq2_s, q2_out), (wk2_bf, bk2_s, k2_out)):
            for c in range(NCH):
                for dk in range(NCH):
                    p.mm(ps_w[:, :], w_bf[:, dk * 1024 + c * 128: dk * 1024 + (c + 1) * 128],
                         xn2s[:, dk * 512:(dk + 1) * 512], dk == 0, dk == 7,
                         waits=(list(xn_v) + [("v", p.n("v")), ("dgp", p.n("dgp"))]) if dk == 0 else (),
                         inc="t" if dk == 7 else None)
                p.ts("vector", outbf[:, c * 512:(c + 1) * 512], ps_w[:, :], b_s[:, c:c + 1], None,
                     ALU.add, waits=p.at("t"))
                obv = p.at("v")
                hA, hB = 2 * c, 2 * c + 1
                p.dma("gpsimd", out_ap[0:64, hA * TC + ts * 512: hA * TC + (ts + 1) * 512],
                      outbf[0:64, c * 512:(c + 1) * 512], waits=obv)
                p.dma("gpsimd", out_ap[0:64, hB * TC + ts * 512: hB * TC + (ts + 1) * 512],
                      outbf[64:128, c * 512:(c + 1) * 512])
        for tb in range(4):
            tcol = ts * 512 + tb * 128
            for half, psv in ((0, ps_w), (1, ps_w2)):
                for dk in range(NCH):
                    p.mm(psv[:, :], xn2s[:, dk * 512 + tb * 128: dk * 512 + tb * 128 + 128],
                         wv2_bf[:, dk * 1024 + half * 512: dk * 1024 + half * 512 + 512],
                         dk == 0, False,
                         waits=(list(xn_v) + [("v", p.n("v")), ("dgp", p.n("dgp"))]) if dk == 0 else ())
                p.mm(psv[:, :], ones_r[:1, :], bv2_bf[:1, half * 512:(half + 1) * 512],
                     False, True, inc="t")
            vmw = p.at("t")
            for h in range(NH):
                psv = ps_w if h < 8 else ps_w2
                p.copy("vector", vstage[:, h * 65: h * 65 + 64],
                       psv[:, (h % 8) * 64:(h % 8) * 64 + 64], waits=vmw)
            p.dma("gpsimd", v2e_out[tcol:tcol + 128, :], vstage[:, :], waits=[("v", p.n("v"))])

    p.nop("gpsimd", waits=p.at("dgp"))
    p.nop("sync", waits=p.at("dgp", "dsy", "dsc"))
    return p


def host_D(inputs, x1_sh, xn1_sh, q1_sh, k1_sh, v1_sh, gmp_sh):
    k1_full = [np.zeros((64, NH * TB), dtype=BF16NP) for _ in range(2)]
    v1_full = [np.zeros((TB, NH * 65), dtype=BF16NP) for _ in range(2)]
    for core in range(NCORES):
        for s, (b, pos0) in enumerate(strip_list(core)):
            for h in range(NH):
                k1_full[b][:, h * TB + pos0: h * TB + pos0 + 512] = \
                    k1_sh[core][:, h * TC + s * 512: h * TC + (s + 1) * 512]
            v1_full[b][pos0:pos0 + 512, :] = v1_sh[core][s * 512:(s + 1) * 512, :]
    dmask = make_dmask()
    gmp_all = np.concatenate(gmp_sh, axis=1)
    wo1T = fmT(inputs["L1_attn_wo"]).astype(np.float32)
    cw = [np.ascontiguousarray(inputs[f"L1_ffn_cn_w{i}"]).astype(np.float32) for i in range(1, 5)]
    cb = [np.asarray(inputs[f"L1_ffn_cn_b{i}"]).astype(np.float32) for i in range(1, 5)]
    gw = [np.ascontiguousarray(inputs[f"L1_ffn_gc_w{i}"]).astype(np.float32) for i in range(1, 4)]
    gb = [np.asarray(inputs[f"L1_ffn_gc_b{i}"]).astype(np.float32) for i in range(1, 4)]
    ar = np.asarray(inputs["L1_ffn_adaptive_range"]).reshape(1, 1).astype(np.float32)
    wq2T = fmT(inputs["L2_attn_Wq"]).astype(np.float32)
    wk2T = fmT(inputs["L2_attn_Wk"]).astype(np.float32)
    wv2T = fmT(inputs["L2_attn_Wv"]).astype(np.float32)
    bcol = lambda v, n: np.ascontiguousarray(np.asarray(v).reshape(n, 128).T).astype(np.float32)
    maps = []
    for core in range(NCORES):
        k1r = np.zeros((NST * 64, NH * NKB * 128), dtype=BF16NP)
        v1r = np.zeros((NST * NKB * 128, NH * 65), dtype=BF16NP)
        for s, (b, pos0) in enumerate(strip_list(core)):
            for h in range(NH):
                k1r[s * 64:(s + 1) * 64, h * NKB * 128: h * NKB * 128 + 512] = \
                    k1_full[b][:, h * TB + pos0: h * TB + pos0 + 512]
                k1r[s * 64:(s + 1) * 64, h * NKB * 128 + 512: h * NKB * 128 + 512 + pos0] = \
                    k1_full[b][:, h * TB: h * TB + pos0]
            v1r[s * NKB * 128: s * NKB * 128 + 512, :] = v1_full[b][pos0:pos0 + 512, :]
            v1r[s * NKB * 128 + 512: s * NKB * 128 + 512 + pos0, :] = v1_full[b][0:pos0, :]
        maps.append({
            "x1": x1_sh[core], "xn1": xn1_sh[core], "q1": q1_sh[core],
            "k1r": k1r, "v1r": v1r, "dmask": dmask.astype(BF16NP), "gmp": gmp_all, "ar": ar,
            "wo1T": wo1T,
            "cw1": cw[0], "cw2": cw[1], "cw3": cw[2], "cw4": cw[3],
            "cb1": bcol(cb[0], 8), "cb2": bcol(cb[1], 4), "cb3": bcol(cb[2], 2),
            "cb4": cb[3].reshape(1, 1),
            "gw1": gw[0], "gw2": gw[1], "gw3": gw[2],
            "gb1": bcol(gb[0], 4), "gb2": bcol(gb[1], 2), "gb3": gb[2].reshape(1, 1),
            "wq2T": wq2T, "wk2T": wk2T, "wv2T": wv2T,
            "bq2": bcol(inputs["L2_attn_bq"], 8), "bk2": bcol(inputs["L2_attn_bk"], 8),
            "bv2": np.asarray(inputs["L2_attn_bv"]).reshape(1, D).astype(np.float32),
        })
    return maps



def build_E():
    p = Prog("mkE")
    nc = p.nc
    q2_in = nc.dram_tensor("q2", [64, NH * TC], BF16, kind="ExternalInput").ap()
    kb_in = nc.dram_tensor("kband", [NST * 64, NH * 768], BF16, kind="ExternalInput").ap()
    vb_in = nc.dram_tensor("vband", [NST * 768, NH * 65], BF16, kind="ExternalInput").ap()
    bmask_in = nc.dram_tensor("bmask", [128, 6 * 512], BF16, kind="ExternalInput").ap()
    gk_in = nc.dram_tensor("gk", [64, 2 * NH * 128], BF16, kind="ExternalInput").ap()
    gv_in = nc.dram_tensor("gv", [128, 2 * NH * 65], BF16, kind="ExternalInput").ap()
    gq_in = nc.dram_tensor("gq", [64, NH * 128], BF16, kind="ExternalInput").ap()
    krow_in = nc.dram_tensor("krow", [64, NH * TB], BF16, kind="ExternalInput").ap()
    vrow_in = nc.dram_tensor("vrow", [TB, NH * 65], BF16, kind="ExternalInput").ap()
    ctxp_out = nc.dram_tensor("ctxp", [D, TC], BF16, kind="ExternalOutput").ap()
    rowc_out = nc.dram_tensor("rowc", [D, 128], BF16, kind="ExternalOutput").ap()

    q2s = nc.alloc_sbuf_tensor("q2s", [128, NH * 512], BF16)
    kbs = nc.alloc_sbuf_tensor("kbs", [128, NH * 768], BF16)
    vbs = nc.alloc_sbuf_tensor("vbs", [128, 6 * NH * 65], BF16)
    gks = nc.alloc_sbuf_tensor("gks", [128, 2 * NH * 128], BF16)
    gvs = nc.alloc_sbuf_tensor("gvs", [128, 2 * NH * 65], BF16)
    gqs = nc.alloc_sbuf_tensor("gqs", [128, NH * 128], BF16)
    kr = nc.alloc_sbuf_tensor("kr", [128, 2 * NH * 1024], BF16)
    vr = nc.alloc_sbuf_tensor("vr", [128, 8 * NH * 65], BF16)
    bmask_s = nc.alloc_sbuf_tensor("bmask_s", [128, 6 * 512], BF16)
    probs = nc.alloc_sbuf_tensor("probs", [128, 2 * 512], BF16)
    ctxbf = nc.alloc_sbuf_tensor("ctxbf", [128, NCH * 512], BF16)
    rowbf = nc.alloc_sbuf_tensor("rowbf", [128, NH * 64], BF16)
    fsml = nc.alloc_sbuf_tensor("fsml", [128, 2048], F32)
    ones_rf = nc.alloc_sbuf_tensor("ones_rf", [1, 128], F32)

    bc1 = fsml[:, 0:512]; bc2 = fsml[:, 512:1024]
    t1f = fsml[:, 1024:1536]
    invs = fsml[:1, 1536:2048]

    ps_a = nc.alloc_psum_tensor("ps_a", [128, 512], F32)
    ps_b = nc.alloc_psum_tensor("ps_b", [128, 512], F32)
    ps_l = nc.alloc_psum_tensor("ps_l", [128, 512], F32)
    ps_c = nc.alloc_psum_tensor("ps_c", [128, 512], F32)
    ps_bc = nc.alloc_psum_tensor("ps_bc", [128, 512], F32)
    ps_r = nc.alloc_psum_tensor("ps_r", [128, 128], F32)
    ps_sc = [ps_a, ps_b]

    p.memset("vector", ones_rf[:, :], 1.0)
    p.dma("sync", bmask_s[:, :], bmask_in)
    p.dma("sync", gks[0:64, :], gk_in)
    p.dma("sync", gvs[:, :], gv_in)
    p.dma("sync", gqs[0:64, :], gq_in)
    ild = p.at("dsy")

    for ts in range(NST):
        bidx = 0 if ts < 2 else 1
        p.dma("scalar", q2s[0:64, :].rearrange("q (h t) -> q h t", t=512),
              q2_in.rearrange("q (h t) -> q h t", t=TC)[:, :, ts * 512:(ts + 1) * 512],
              waits=[("t", p.n("t"))])
        p.dma("sync", kbs[0:64, :], kb_in[ts * 64:(ts + 1) * 64, :],
              waits=[("t", p.n("t")), ("s", p.n("s"))] + (list(ild) if ts == 0 else []))
        p.dma("sync", vbs[:, :].rearrange("p (cc c) -> p cc c", c=NH * 65),
              vb_in[ts * 768:(ts + 1) * 768, :].rearrange("(cc p) c -> p cc c", p=128))
        sld = p.at("dsy", "dsc")
        for h in range(NH):
            qh = q2s[0:64, h * 512:(h + 1) * 512]
            score_t = {}
            hd_v = p.at("v")
            for cc in range(2):
                p.mm(ps_sc[cc][:, :], kbs[0:64, h * 768 + cc * 128: h * 768 + (cc + 1) * 128],
                     qh, True, True, waits=list(sld) + [("s", p.n("s")), ("v", p.n("v"))], inc="t")
                score_t[cc] = p.n("t")
            for cc in range(6):
                sl2 = cc % 2
                p.tt("vector", ps_sc[sl2][:, :], ps_sc[sl2][:, :],
                     bmask_s[:, cc * 512:(cc + 1) * 512], ALU.add, waits=[("t", score_t[cc])])
                p.act(probs[:, sl2 * 512:(sl2 + 1) * 512], ps_sc[sl2][:, :], AF.Exp,
                      bias=float(EXPB), scale=float(SC8), waits=[("v", p.n("v"))])
                ecnt = p.n("s")
                p.mm(ps_l[0:65, :], vbs[:, cc * NH * 65 + h * 65: cc * NH * 65 + h * 65 + 65],
                     probs[:, sl2 * 512:(sl2 + 1) * 512], cc == 0, cc == 5,
                     waits=[("s", ecnt)] + (hd_v if cc == 0 else []),
                     inc="t" if cc == 5 else None)
                if cc + 2 < 6:
                    p.mm(ps_sc[sl2][:, :], kbs[0:64, h * 768 + (cc + 2) * 128: h * 768 + (cc + 3) * 128],
                         qh, True, True, waits=[("s", ecnt)], inc="t")
                    score_t[cc + 2] = p.n("t")
            p.mm(ps_c[:, :], gks[0:64, bidx * NH * 128 + h * 128: bidx * NH * 128 + (h + 1) * 128],
                 qh, True, True, waits=[("s", p.n("s")), ("v", p.n("v"))], inc="t")
            p.act(probs[:, 0:512], ps_c[:, :], AF.Exp, bias=float(EXPB), scale=float(SC8),
                  waits=p.at("t"))
            p.mm(ps_c[0:65, :], gvs[:, bidx * NH * 65 + h * 65: bidx * NH * 65 + h * 65 + 65],
                 probs[:, 0:512], True, True, waits=[("s", p.n("s")), ("v", p.n("v"))], inc="t")
            pvq = p.at("t")
            p.recip(invs, ps_l[64:65, :], waits=pvq)
            p.mm(ps_bc[:, :], ones_rf[:, :], invs, True, True, waits=p.at("v"), inc="t")
            p.copy("vector", bc1, ps_bc[:, :], waits=p.at("t"))
            p.recip(invs, ps_c[64:65, :])
            p.mm(ps_bc[:, :], ones_rf[:, :], invs, True, True, waits=p.at("v"), inc="t")
            p.copy("vector", bc2, ps_bc[:, :], waits=p.at("t"))
            hb = (h % 2) * 64
            p.tt("vector", t1f[hb:hb + 64, :], ps_l[0:64, :], bc1[hb:hb + 64, :], ALU.mult)
            dst = ctxbf[hb:hb + 64, (h // 2) * 512:(h // 2) * 512 + 512]
            p.tt("vector", dst, ps_c[0:64, :], bc2[hb:hb + 64, :], ALU.mult,
                 waits=[("dgp", p.n("dgp"))])
            p.tt("vector", dst, dst, t1f[hb:hb + 64, :], ALU.add)
        for c in range(NCH):
            p.dma("gpsimd", ctxp_out[c * 128:(c + 1) * 128, ts * 512:(ts + 1) * 512],
                  ctxbf[:, c * 512:(c + 1) * 512], waits=p.at("v"))

    local_done = p.at("t", "v", "s")

    for h in range(NH):
        score_t = {}
        for grp in range(8):
            slot = grp % 2
            p.dma("sync", kr[0:64, slot * 1024:(slot + 1) * 1024],
                  krow_in[:, h * TB + grp * 1024: h * TB + (grp + 1) * 1024],
                  waits=[("t", p.n("t")), ("s", p.n("s"))] + (list(local_done) if (h == 0 and grp < 2) else []))
            p.dma("sync", vr[:, slot * 8 * 65:(slot + 1) * 8 * 65].rearrange("p (kb c) -> p kb c", c=65),
                  vrow_in[grp * 1024:(grp + 1) * 1024, h * 65:(h + 1) * 65]
                      .rearrange("(kb p) c -> p kb c", p=128))
            gld = p.at("dsy")
            qh = gqs[0:64, h * 128:(h + 1) * 128]
            for kb in range(8):
                sl2 = kb % 2
                p.mm(ps_sc[sl2][:, 0:128], kr[0:64, slot * 1024 + kb * 128: slot * 1024 + (kb + 1) * 128],
                     qh, True, True, waits=list(gld) + [("s", p.n("s"))], inc="t")
                p.act(probs[:, sl2 * 512:sl2 * 512 + 128], ps_sc[sl2][:, 0:128], AF.Exp,
                      bias=float(EXPB), scale=float(SC8), waits=[("t", p.n("t"))])
                p.mm(ps_r[0:65, :], vr[:, slot * 8 * 65 + kb * 65: slot * 8 * 65 + kb * 65 + 65],
                     probs[:, sl2 * 512:sl2 * 512 + 128], grp == 0 and kb == 0,
                     grp == 7 and kb == 7,
                     waits=[("s", p.n("s"))] + ([("v", p.n("v"))] if (grp == 0 and kb == 0) else []),
                     inc="t" if (grp == 7 and kb == 7) else None)
        p.recip(invs[:1, 0:128], ps_r[64:65, :], waits=p.at("t"))
        p.mm(ps_bc[:, 0:128], ones_rf[:, :], invs[:1, 0:128], True, True,
             waits=p.at("v"), inc="t")
        p.copy("vector", bc1[:, 0:128], ps_bc[:, 0:128], waits=p.at("t"))
        p.tt("vector", rowbf[(h % 2) * 64:(h % 2) * 64 + 64, (h // 2) * 128:(h // 2) * 128 + 128],
             ps_r[0:64, :], bc1[(h % 2) * 64:(h % 2) * 64 + 64, 0:128], ALU.mult)
    p.dma("gpsimd", rowc_out.rearrange("(c p) t -> p c t", p=128),
          rowbf[:, :].rearrange("p (c t) -> p c t", t=128), waits=p.at("v"))

    p.nop("gpsimd", waits=p.at("dgp"))
    p.nop("sync", waits=p.at("dgp", "dsy", "dsc"))
    return p


def make_bmask():
    m = np.zeros((128, 6 * 512), dtype=np.float32)
    kk = np.arange(128)[:, None]
    qq = np.arange(512)[None, :]
    qb = (qq // 64) * 64
    for cc in range(6):
        kpos = -128 + 128 * cc + kk
        bad = (kpos < qb - 64) | (kpos >= qb + 128)
        m[:, cc * 512:(cc + 1) * 512] = np.where(bad, NEGC, 0.0)
    return m.astype(BF16NP)


def host_E(inputs, q2_sh, k2_sh, v2_sh):
    gi = [int(v) for v in np.asarray(inputs["L2_attn_global_indices"]).reshape(-1)]
    k2_full = [np.zeros((64, NH * TB), dtype=BF16NP) for _ in range(2)]
    v2_full = [np.zeros((TB, NH * 65), dtype=BF16NP) for _ in range(2)]
    q2_full = [np.zeros((64, NH * TB), dtype=BF16NP) for _ in range(2)]
    for core in range(NCORES):
        for s, (b, pos0) in enumerate(strip_list(core)):
            for h in range(NH):
                k2_full[b][:, h * TB + pos0: h * TB + pos0 + 512] = \
                    k2_sh[core][:, h * TC + s * 512: h * TC + (s + 1) * 512]
                q2_full[b][:, h * TB + pos0: h * TB + pos0 + 512] = \
                    q2_sh[core][:, h * TC + s * 512: h * TC + (s + 1) * 512]
            v2_full[b][pos0:pos0 + 512, :] = v2_sh[core][s * 512:(s + 1) * 512, :]
    bmask = make_bmask()
    gk = [np.zeros((64, 2 * NH * 128), dtype=BF16NP) for _ in range(2)]
    gkb = np.zeros((64, 2 * NH * 128), dtype=BF16NP)
    gvb = np.zeros((128, 2 * NH * 65), dtype=BF16NP)
    for b in range(2):
        for h in range(NH):
            for ui, g in enumerate(gi):
                gkb[:, b * NH * 128 + h * 128 + ui * 64: b * NH * 128 + h * 128 + (ui + 1) * 64] = \
                    k2_full[b][:, h * TB + g * 64:h * TB + (g + 1) * 64]
            gvb[0:64, b * NH * 65 + h * 65: b * NH * 65 + h * 65 + 65] = \
                v2_full[b][gi[0] * 64:(gi[0] + 1) * 64, h * 65: h * 65 + 65]
            gvb[64:128, b * NH * 65 + h * 65: b * NH * 65 + h * 65 + 65] = \
                v2_full[b][gi[1] * 64:(gi[1] + 1) * 64, h * 65: h * 65 + 65]
    maps = []
    for core in range(NCORES):
        kband = np.zeros((NST * 64, NH * 768), dtype=BF16NP)
        vband = np.zeros((NST * 768, NH * 65), dtype=BF16NP)
        for s, (b, pos0) in enumerate(strip_list(core)):
            lo = pos0 - 128
            for cc in range(6):
                ks = (lo + cc * 128) % TB
                for h in range(NH):
                    kband[s * 64:(s + 1) * 64, h * 768 + cc * 128: h * 768 + (cc + 1) * 128] = \
                        k2_full[b][:, h * TB + ks: h * TB + ks + 128]
                vband[s * 768 + cc * 128: s * 768 + (cc + 1) * 128, :] = v2_full[b][ks:ks + 128, :]
        br = core % 2
        gq = np.zeros((64, NH * 128), dtype=BF16NP)
        for h in range(NH):
            for ui, g in enumerate(gi):
                gq[:, h * 128 + ui * 64: h * 128 + (ui + 1) * 64] = \
                    q2_full[br][:, h * TB + g * 64: h * TB + (g + 1) * 64]
        maps.append({
            "q2": q2_sh[core], "kband": kband, "vband": vband, "bmask": bmask,
            "gk": gkb, "gv": gvb, "gq": gq,
            "krow": k2_full[br], "vrow": v2_full[br],
        })
    return maps



def build_F():
    p = Prog("mkF")
    nc = p.nc
    x2_in = nc.dram_tensor("x2", [D, TC], F32, kind="ExternalInput").ap()
    xn2_in = nc.dram_tensor("xn2", [D, TC], BF16, kind="ExternalInput").ap()
    ctxp_in = nc.dram_tensor("ctxp", [D, TC], BF16, kind="ExternalInput").ap()
    rpad_in = nc.dram_tensor("rpad", [D, TC], BF16, kind="ExternalInput").ap()
    rself_in = nc.dram_tensor("rself", [1, TC], F32, kind="ExternalInput").ap()
    wo2T_in = nc.dram_tensor("wo2T", [D, D], F32, kind="ExternalInput").ap()
    bo2_in = nc.dram_tensor("bo2", [128, 8], F32, kind="ExternalInput").ap()
    tw1_in = nc.dram_tensor("tw1", [D, 4096], F32, kind="ExternalInput").ap()
    tw2_in = nc.dram_tensor("tw2", [D, 4096], F32, kind="ExternalInput").ap()
    th1_in = nc.dram_tensor("th1", [4096, 4096], F32, kind="ExternalInput").ap()
    th2_in = nc.dram_tensor("th2", [4096, 4096], F32, kind="ExternalInput").ap()
    tw3_in = nc.dram_tensor("tw3", [4096, D], F32, kind="ExternalInput").ap()
    s1_in = nc.dram_tensor("s1", [128, 32], F32, kind="ExternalInput").ap()
    s2_in = nc.dram_tensor("s2", [128, 32], F32, kind="ExternalInput").ap()
    sh1_in = nc.dram_tensor("sh1", [128, 32], F32, kind="ExternalInput").ap()
    sh2_in = nc.dram_tensor("sh2", [128, 32], F32, kind="ExternalInput").ap()
    s3_in = nc.dram_tensor("s3", [128, 8], F32, kind="ExternalInput").ap()
    x3_out = nc.dram_tensor("x3", [D, TC], F32, kind="ExternalOutput").ap()
    x3a_scr = nc.dram_tensor("x3a_scr", [D, TC], F32).ap()

    AR1 = nc.alloc_sbuf_tensor("AR1", [128, 32768], BF16)
    AR2 = nc.alloc_sbuf_tensor("AR2", [128, 32768], BF16)
    ARF = nc.alloc_sbuf_tensor("ARF", [128, 8192], F32)
    stage = nc.alloc_sbuf_tensor("stage", [128, 4096], F32)
    misc = nc.alloc_sbuf_tensor("misc", [128, 3072], F32)
    wt = nc.alloc_sbuf_tensor("wt", [128, 4096], BF16)
    ones_rf = nc.alloc_sbuf_tensor("ones_rf", [1, 128], F32)

    ps1 = nc.alloc_psum_tensor("ps1", [128, 512], F32)
    ps2 = nc.alloc_psum_tensor("ps2", [128, 512], F32)
    ps3 = nc.alloc_psum_tensor("ps3", [128, 512], F32)
    ps4 = nc.alloc_psum_tensor("ps4", [128, 512], F32)

    s1_s = misc[:, 0:32]; s2_s = misc[:, 32:64]; sh1_s = misc[:, 64:96]; sh2_s = misc[:, 96:128]
    s3_s = misc[:, 128:136]; bo2_s = misc[:, 136:144]
    rs_bc = misc[:, 2048:2560]
    scr = misc[:, 2560:3072]

    p.memset("vector", ones_rf[:, :], 1.0)
    p.dma("sync", stage[:, 0:32], s1_in)
    p.dma("sync", stage[:, 32:64], s2_in)
    p.dma("sync", stage[:, 64:96], sh1_in)
    p.dma("sync", stage[:, 96:128], sh2_in)
    p.dma("sync", stage[:, 128:136], s3_in)
    p.dma("sync", stage[:, 136:144], bo2_in)
    p.copy("vector", misc[:, 0:144], stage[:, 0:144], waits=p.at("dsy"))
    sld = p.at("v")

    wo2_bf = AR1[:, 0:8192]
    src3 = wo2T_in.rearrange("(c p) e -> p c e", p=128)
    for c0 in range(0, NCH, 2):
        p.dma("scalar", stage[:, 0:2 * D].rearrange("p (c e) -> p c e", e=D),
              src3[:, c0:c0 + 2, :], waits=[("v", p.n("v"))])
        p.copy("vector", wo2_bf[:, c0 * D:(c0 + 2) * D], stage[:, 0:2 * D], waits=p.at("dsc"))
    wo2_v = p.at("v")

    ctxs = AR2[:, 0:4096]
    rpds = AR2[:, 4096:8192]
    ctxf = AR2[:, 8192:12288]
    x2s = ARF[:, 0:4096]
    for ts in range(NST):
        p.dma("sync", ctxs.rearrange("p (c t) -> p c t", t=512),
              ctxp_in[:, ts * 512:(ts + 1) * 512].rearrange("(c p) t -> p c t", p=128),
              waits=[("t", p.n("t")), ("v", p.n("v"))])
        p.dma("sync", rpds.rearrange("p (c t) -> p c t", t=512),
              rpad_in[:, ts * 512:(ts + 1) * 512].rearrange("(c p) t -> p c t", p=128))
        p.dma("scalar", stage[:1, 2048:2560], rself_in[:, ts * 512:(ts + 1) * 512],
              waits=[("v", p.n("v"))])
        p.dma("gpsimd", x2s.rearrange("p (c t) -> p c t", t=512),
              x2_in[:, ts * 512:(ts + 1) * 512].rearrange("(c p) t -> p c t", p=128),
              waits=[("v", p.n("v"))])
        cld = p.at("dsy", "dsc", "dgp")
        p.mm(ps1[:, :], ones_rf[:, :], stage[:1, 2048:2560], True, True,
             waits=cld + [("v", p.n("v"))], inc="t")
        p.copy("vector", rs_bc, ps1[:, :], waits=p.at("t"))
        for c in range(NCH):
            p.tt("vector", ctxf[:, c * 512:(c + 1) * 512], ctxs[:, c * 512:(c + 1) * 512],
                 rs_bc, ALU.mult, waits=cld)
            p.tt("vector", ctxf[:, c * 512:(c + 1) * 512], ctxf[:, c * 512:(c + 1) * 512],
                 rpds[:, c * 512:(c + 1) * 512], ALU.add)
        ctx_v = p.at("v")
        for oc in range(NCH):
            for dk in range(NCH):
                p.mm(ps2[:, :], wo2_bf[:, dk * D + oc * 128: dk * D + (oc + 1) * 128],
                     ctxf[:, dk * 512:(dk + 1) * 512], dk == 0, dk == NCH - 1,
                     waits=(list(wo2_v) + list(ctx_v) + [("v", p.n("v"))]) if dk == 0 else (),
                     inc="t" if dk == NCH - 1 else None)
            xsl = x2s[:, oc * 512:(oc + 1) * 512]
            p.ts("vector", scr, ps2[:, :], bo2_s[:, oc:oc + 1], None, ALU.add, waits=p.at("t"))
            p.tt("vector", xsl, xsl, scr, ALU.add)
        p.dma("gpsimd", x3a_scr.rearrange("(c p) t -> p c t", p=128)[:, :, ts * 512:(ts + 1) * 512],
              x2s.rearrange("p (c t) -> p c t", t=512), waits=p.at("v"))
    f1_done = p.at("t", "v", "s", "dgp")

    bufA = AR1[:, 0:32768]
    bufB = AR2[:, 0:32768]
    xn2s = ARF[:, 0:4096].bitcast(BF16)

    def stream_mm(ps_list, w_in, Kdim, oc, xsrc, extra_waits):
        nkc = Kdim // 128
        piece = 16
        for p0 in range(0, nkc, piece):
            n = min(piece, nkc - p0)
            slot = (p0 // piece) % 2
            p.dma("scalar", stage[:, slot * 2048: slot * 2048 + n * 128]
                      .rearrange("p (c e) -> p c e", e=128),
                  w_in.rearrange("(c p) e -> p c e", p=128)[:, p0:p0 + n, oc * 128:(oc + 1) * 128],
                  waits=[("v", p.n("v"))])
            p.copy("vector", wt[:, slot * 2048: slot * 2048 + n * 128],
                   stage[:, slot * 2048: slot * 2048 + n * 128],
                   waits=p.at("dsc") + [("t", p.n("t"))])
            wv = p.at("v")
            for dk in range(p0, p0 + n):
                for th in range(len(ps_list)):
                    p.mm(ps_list[th][:, :], wt[:, slot * 2048 + (dk - p0) * 128:
                                                 slot * 2048 + (dk - p0) * 128 + 128],
                         xsrc(dk, th), dk == 0, dk == nkc - 1,
                         waits=(list(wv) + list(extra_waits) + [("s", p.n("s")), ("v", p.n("v"))])
                         if (dk == p0 and th == 0) else (),
                         inc="t" if (dk == nkc - 1 and th == len(ps_list) - 1) else None)

    for fs in range(2):
        p.dma("sync", xn2s.rearrange("p (c t) -> p c t", t=1024),
              xn2_in[:, fs * 1024:(fs + 1) * 1024].rearrange("(c p) t -> p c t", p=128),
              waits=[("t", p.n("t")), ("v", p.n("v"))] + (f1_done if fs == 0 else []))
        xld = p.at("dsy")
        xsrc_x = lambda dk, th: xn2s[:, dk * 1024 + th * 512: dk * 1024 + th * 512 + 512]
        for oc in range(32):
            stream_mm([ps1, ps2], tw1_in, 1024, oc, xsrc_x, xld)
            for th, psx in ((0, ps1), (1, ps2)):
                p.act(bufA[:, oc * 1024 + th * 512: oc * 1024 + th * 512 + 512], psx[:, :],
                      AF.Silu, scale=s1_s[:, oc:oc + 1], waits=p.at("t"))
        for oc in range(32):
            stream_mm([ps3, ps4], tw2_in, 1024, oc, xsrc_x, xld)
            for th, psx in ((0, ps3), (1, ps4)):
                p.ts("vector", bufB[:, oc * 1024 + th * 512: oc * 1024 + th * 512 + 512],
                     psx[:, :], s2_s[:, oc:oc + 1], None, ALU.mult, waits=p.at("t"))
        hv = p.at("v", "s")
        for oc in range(32):
            p.tt("vector", bufB[:, oc * 1024:(oc + 1) * 1024], bufA[:, oc * 1024:(oc + 1) * 1024],
                 bufB[:, oc * 1024:(oc + 1) * 1024], ALU.mult, waits=hv)
        h_v = p.at("v")
        xsrc_B = lambda dk, th: bufB[:, dk * 1024 + th * 512: dk * 1024 + th * 512 + 512]
        for oc in range(32):
            stream_mm([ps1, ps2], th1_in, 4096, oc, xsrc_B, h_v)
            for th, psx in ((0, ps1), (1, ps2)):
                p.act(bufA[:, oc * 1024 + th * 512: oc * 1024 + th * 512 + 512], psx[:, :],
                      AF.Gelu, scale=sh1_s[:, oc:oc + 1], waits=p.at("t"))
        h1_v = p.at("s")
        xsrc_A = lambda dk, th: bufA[:, dk * 1024 + th * 512: dk * 1024 + th * 512 + 512]
        for oc in range(32):
            stream_mm([ps3, ps4], th2_in, 4096, oc, xsrc_A, h1_v)
            for th, psx in ((0, ps3), (1, ps4)):
                p.act(bufB[:, oc * 1024 + th * 512: oc * 1024 + th * 512 + 512], psx[:, :],
                      AF.Gelu, scale=sh2_s[:, oc:oc + 1], waits=p.at("t"))
        h2_v = p.at("s")
        for sub in range(2):
            ts = fs * 2 + sub
            x3sv = ARF[:, 4096:8192]
            p.dma("sync", x3sv.rearrange("p (c t) -> p c t", t=512),
                  x3a_scr.rearrange("(c p) t -> p c t", p=128)[:, :, ts * 512:(ts + 1) * 512],
                  waits=[("t", p.n("t")), ("v", p.n("v")), ("s", p.n("s")), ("dgp", p.n("dgp"))])
            x3ld = p.at("dsy")
            for oc in range(NCH):
                stream_mm([ps1], tw3_in, 4096, oc,
                          lambda dk, th, _s=sub: bufB[:, dk * 1024 + _s * 512: dk * 1024 + _s * 512 + 512],
                          h2_v)
                p.ts("vector", scr, ps1[:, :], s3_s[:, oc:oc + 1], None, ALU.mult,
                     waits=p.at("t"))
                xsl = x3sv[:, oc * 512:(oc + 1) * 512]
                p.tt("vector", xsl, xsl, scr, ALU.add, waits=x3ld)
            p.dma("gpsimd", x3_out.rearrange("(c p) t -> p c t", p=128)[:, :, ts * 512:(ts + 1) * 512],
                  x3sv.rearrange("p (c t) -> p c t", t=512), waits=p.at("v"))

    p.nop("gpsimd", waits=p.at("dgp"))
    p.nop("sync", waits=p.at("dgp", "dsy", "dsc"))
    return p


def host_F(inputs, x2_sh, xn2_sh, ctxp_sh, rowc_sh):
    gi = [int(v) for v in np.asarray(inputs["L2_attn_global_indices"]).reshape(-1)]
    wo2T = fmT(inputs["L2_attn_Wo"]).astype(np.float32)
    bo2 = np.ascontiguousarray(np.asarray(inputs["L2_attn_bo"]).reshape(8, 128).T).astype(np.float32)
    tw1 = fmT(inputs["L2_ffn_w1_w"]).astype(np.float32)
    tw2 = fmT(inputs["L2_ffn_w2_w"]).astype(np.float32)
    th1 = fmT(inputs["L2_ffn_h1_w"]).astype(np.float32)
    th2 = fmT(inputs["L2_ffn_h2_w"]).astype(np.float32)
    tw3 = fmT(inputs["L2_ffn_w3_w"]).astype(np.float32)
    col = lambda v, n: np.ascontiguousarray(np.asarray(v).reshape(n, 128).T).astype(np.float32)
    s1 = col(inputs["L2_ffn_w1_s"], 32); s2 = col(inputs["L2_ffn_w2_s"], 32)
    sh1 = col(inputs["L2_ffn_h1_s"], 32); sh2 = col(inputs["L2_ffn_h2_s"], 32)
    s3 = col(inputs["L2_ffn_w3_s"], 8)
    maps = []
    for core in range(NCORES):
        rpad = np.zeros((D, TC), dtype=BF16NP)
        rself = np.ones((1, TC), dtype=np.float32)
        for s, (b, pos0) in enumerate(strip_list(core)):
            for ui, g in enumerate(gi):
                gpos = g * 64
                if pos0 <= gpos < pos0 + 512:
                    srccore = b
                    rc = rowc_sh[srccore]
                    loc = s * 512 + (gpos - pos0)
                    rpad[:, loc:loc + 64] = rc[:, ui * 64:(ui + 1) * 64]
                    rself[0, loc:loc + 64] = 0.0
        maps.append({
            "x2": x2_sh[core], "xn2": xn2_sh[core], "ctxp": ctxp_sh[core],
            "rpad": rpad, "rself": rself,
            "wo2T": wo2T, "bo2": bo2,
            "tw1": tw1, "tw2": tw2, "th1": th1, "th2": th2, "tw3": tw3,
            "s1": s1, "s2": s2, "sh1": sh1, "sh2": sh2, "s3": s3,
        })
    return maps



def build_G():
    p = Prog("mkG")
    nc = p.nc
    x3_in = nc.dram_tensor("x3", [D, TC], F32, kind="ExternalInput").ap()
    wq3T_in = nc.dram_tensor("wq3T", [D, D], F32, kind="ExternalInput").ap()
    wk3T_in = nc.dram_tensor("wk3T", [D, D], F32, kind="ExternalInput").ap()
    wv3T_in = nc.dram_tensor("wv3T", [D, D], F32, kind="ExternalInput").ap()
    bq3_in = nc.dram_tensor("bq3", [128, 8], F32, kind="ExternalInput").ap()
    bk3_in = nc.dram_tensor("bk3", [1, D], F32, kind="ExternalInput").ap()
    bv3_in = nc.dram_tensor("bv3", [1, D], F32, kind="ExternalInput").ap()
    xn3_out = nc.dram_tensor("xn3", [D, TC], BF16, kind="ExternalOutput").ap()
    q3_out = nc.dram_tensor("q3", [64, NH * TC], BF16, kind="ExternalOutput").ap()
    kvp_out = nc.dram_tensor("kvp", [65, 2 * NH * 65], F32, kind="ExternalOutput").ap()

    x_sb = nc.alloc_sbuf_tensor("x_sb", [128, NCH * TC], F32)
    xn_sb = nc.alloc_sbuf_tensor("xn_sb", [128, NCH * TC], BF16)
    wq_bf = nc.alloc_sbuf_tensor("wq_bf", [128, NCH * D], BF16)
    wk_bf = nc.alloc_sbuf_tensor("wk_bf", [128, NCH * D], BF16)
    wv_bf = nc.alloc_sbuf_tensor("wv_bf", [128, NCH * D], BF16)
    stage = nc.alloc_sbuf_tensor("stage", [128, 4096], F32)
    k3e = nc.alloc_sbuf_tensor("k3e", [128, NH * 65], BF16)
    v3e = nc.alloc_sbuf_tensor("v3e", [128, NH * 65], BF16)
    kvacc = nc.alloc_sbuf_tensor("kvacc", [128, 2 * NH * 65], F32)
    qsc = nc.alloc_sbuf_tensor("qsc", [128, 512], BF16)
    e1 = nc.alloc_sbuf_tensor("e1", [128, 512], F32)
    e2 = nc.alloc_sbuf_tensor("e2", [128, 512], F32)
    biases = nc.alloc_sbuf_tensor("biases", [128, 8], F32)
    brow = nc.alloc_sbuf_tensor("brow", [1, 2048], BF16)
    ones_r = nc.alloc_sbuf_tensor("ones_r", [1, 128], BF16)
    B = LNBufs(nc, "ln")
    ps_q = nc.alloc_psum_tensor("ps_q", [128, 512], F32)
    ps_v0 = nc.alloc_psum_tensor("ps_v0", [128, 512], F32)
    ps_v1 = nc.alloc_psum_tensor("ps_v1", [128, 512], F32)
    ps_kv = nc.alloc_psum_tensor("ps_kv", [128, 128], F32)

    B.init(p)
    p.memset("vector", ones_r[:, :], 1.0)
    p.memset("vector", kvacc[:, :], 0.0)
    p.dma("sync", x_sb[:, :].rearrange("p (c t) -> p c t", c=NCH),
          x3_in.rearrange("(c p) t -> p c t", p=128))
    xld = p.at("dsy")
    for w_in, w_bf in ((wq3T_in, wq_bf), (wk3T_in, wk_bf), (wv3T_in, wv_bf)):
        src3 = w_in.rearrange("(c p) e -> p c e", p=128)
        for c0 in range(0, NCH, 4):
            p.dma("scalar", stage[:, 0:4 * D].rearrange("p (c e) -> p c e", e=D),
                  src3[:, c0:c0 + 4, :], waits=[("v", p.n("v"))])
            p.copy("vector", w_bf[:, c0 * D:(c0 + 4) * D], stage[:, 0:4 * D], waits=p.at("dsc"))
    p.dma("scalar", stage[:, 0:8], bq3_in, waits=[("v", p.n("v"))])
    p.dma("scalar", stage[:1, 8:1032], bk3_in)
    p.dma("scalar", stage[:1, 1032:2056], bv3_in)
    p.copy("vector", biases[:, :], stage[:, 0:8], waits=p.at("dsc"))
    p.copy("vector", brow[:1, 0:1024], stage[:1, 8:1032])
    p.copy("vector", brow[:1, 1024:2048], stage[:1, 1032:2056])
    wld = p.at("v")

    for ts in range(NST):
        bsel = 0 if ts < 2 else 1
        x_of_chunk = lambda c, _ts=ts: x_sb[:, c * TC + _ts * 512: c * TC + (_ts + 1) * 512]

        def xn_cb(c, xf, waits, _ts=ts):
            dst_b = xn_sb[:, c * TC + _ts * 512: c * TC + (_ts + 1) * 512]
            p.copy("vector", dst_b, xf)
            p.dma("gpsimd", xn3_out[c * 128:(c + 1) * 128, _ts * 512:(_ts + 1) * 512],
                  dst_b, waits=[("v", p.n("v"))])

        emit_ln_strip(p, B, x_of_chunk, ts, xn_cb, x_ready=xld)
        xn_v = p.at("v")
        for c in range(NCH):
            for dk in range(NCH):
                p.mm(ps_q[:, :], wq_bf[:, dk * D + c * 128: dk * D + (c + 1) * 128],
                     xn_sb[:, dk * TC + ts * 512: dk * TC + (ts + 1) * 512], dk == 0, dk == NCH - 1,
                     waits=(list(wld) + list(xn_v) + [("v", p.n("v")), ("dgp", p.n("dgp"))]) if dk == 0 else (),
                     inc="t" if dk == NCH - 1 else None)
            p.ts("vector", e1[:, :], ps_q[:, :], biases[:, c:c + 1], 0.0, ALU.add, ALU.min,
                 waits=p.at("t"))
            p.act(e1[:, :], e1[:, :], AF.Exp, waits=p.at("v"))
            p.ts("vector", e2[:, :], ps_q[:, :], biases[:, c:c + 1], 0.0, ALU.add, ALU.max)
            p.tt("vector", qsc[:, :], e1[:, :], e2[:, :], ALU.add, waits=p.at("s"))
            obv = p.at("v")
            hA, hB = 2 * c, 2 * c + 1
            p.dma("gpsimd", q3_out[0:64, hA * TC + ts * 512: hA * TC + (ts + 1) * 512],
                  qsc[0:64, :], waits=obv)
            p.dma("gpsimd", q3_out[0:64, hB * TC + ts * 512: hB * TC + (ts + 1) * 512],
                  qsc[64:128, :])
        for tb in range(4):
            for src_w, boff, dst_ext, feat in ((wk_bf, 0, k3e, True), (wv_bf, 1024, v3e, False)):
                for half in range(2):
                    psx = [ps_v0, ps_v1][half]
                    for dk in range(NCH):
                        p.mm(psx[:, :], xn_sb[:, dk * TC + ts * 512 + tb * 128:
                                              dk * TC + ts * 512 + tb * 128 + 128],
                             src_w[:, dk * D + half * 512: dk * D + half * 512 + 512],
                             dk == 0, False,
                             waits=(list(xn_v) + [("v", p.n("v")), ("t", p.n("t"))]) if dk == 0 else ())
                    p.mm(psx[:, :], ones_r[:1, :], brow[:1, boff + half * 512: boff + (half + 1) * 512],
                         False, True, inc="t")
                vmw = p.at("t")
                for h in range(NH):
                    psx = [ps_v0, ps_v1][h // 8]
                    seg = psx[:, (h % 8) * 64:(h % 8) * 64 + 64]
                    if feat:
                        p.ts("vector", e1[:, 0:64], seg, 0.0, None, ALU.min, waits=vmw)
                        p.act(e1[:, 0:64], e1[:, 0:64], AF.Exp, waits=p.at("v"))
                        p.ts("vector", e2[:, 0:64], seg, 0.0, None, ALU.max)
                        p.tt("vector", dst_ext[:, h * 65: h * 65 + 64], e1[:, 0:64], e2[:, 0:64],
                             ALU.add, waits=p.at("s"))
                    else:
                        p.copy("vector", dst_ext[:, h * 65: h * 65 + 64], seg, waits=vmw)
                p.memset("vector", dst_ext[:, 64:NH * 65:65].rearrange("p h -> p h") if False
                         else dst_ext[:, 0:1], 0.0) if False else None
            for h in range(NH):
                p.memset("vector", k3e[:, h * 65 + 64: h * 65 + 65], 1.0)
                p.memset("vector", v3e[:, h * 65 + 64: h * 65 + 65], 1.0)
            kv_v = p.at("v")
            for h in range(NH):
                p.mm(ps_kv[0:65, 0:65], k3e[:, h * 65: h * 65 + 65], v3e[:, h * 65: h * 65 + 65],
                     True, True, waits=kv_v + [("v", p.n("v"))], inc="t")
                acc_sl = kvacc[0:65, (bsel * NH + h) * 65:(bsel * NH + h + 1) * 65]
                p.tt("vector", acc_sl, acc_sl, ps_kv[0:65, 0:65], ALU.add, waits=p.at("t"))
    p.dma("gpsimd", kvp_out, kvacc[0:65, :], waits=p.at("v"))
    p.nop("gpsimd", waits=p.at("dgp"))
    p.nop("sync", waits=p.at("dgp", "dsy", "dsc"))
    return p


def host_G(inputs, x3_sh):
    wq3T = fmT(inputs["L3_attn_Wq"]).astype(np.float32)
    wk3T = fmT(inputs["L3_attn_Wk"]).astype(np.float32)
    wv3T = fmT(inputs["L3_attn_Wv"]).astype(np.float32)
    bq3 = np.ascontiguousarray(np.asarray(inputs["L3_attn_bq"]).reshape(8, 128).T).astype(np.float32)
    bk3 = np.asarray(inputs["L3_attn_bk"]).reshape(1, D).astype(np.float32)
    bv3 = np.asarray(inputs["L3_attn_bv"]).reshape(1, D).astype(np.float32)
    return [{"x3": x3_sh[core], "wq3T": wq3T, "wk3T": wk3T, "wv3T": wv3T,
             "bq3": bq3, "bk3": bk3, "bv3": bv3} for core in range(NCORES)]



def build_H():
    p = Prog("mkH")
    nc = p.nc
    x3_in = nc.dram_tensor("x3", [D, TC], F32, kind="ExternalInput").ap()
    xn3_in = nc.dram_tensor("xn3", [D, TC], BF16, kind="ExternalInput").ap()
    q3_in = nc.dram_tensor("q3", [64, NH * TC], BF16, kind="ExternalInput").ap()
    kvp_in = nc.dram_tensor("kvp", [65, NCORES * 2 * NH * 65], F32, kind="ExternalInput").ap()
    wo3T_in = nc.dram_tensor("wo3T", [D, D], F32, kind="ExternalInput").ap()
    bo3_in = nc.dram_tensor("bo3", [128, 8], F32, kind="ExternalInput").ap()
    fw1_in = nc.dram_tensor("fw1", [D, 4096], F32, kind="ExternalInput").ap()
    fb1_in = nc.dram_tensor("fb1", [128, 32], F32, kind="ExternalInput").ap()
    fw2_in = nc.dram_tensor("fw2", [4096, D], F32, kind="ExternalInput").ap()
    fb2_in = nc.dram_tensor("fb2", [128, 8], F32, kind="ExternalInput").ap()
    out_out = nc.dram_tensor("out", [D, TC], F32, kind="ExternalOutput").ap()
    bw2_scr = nc.dram_tensor("bw2_scr", [4096, D], BF16).ap()

    kvb = nc.alloc_sbuf_tensor("kvb", [128, 2 * NH * 65], BF16)
    wo3_bf = nc.alloc_sbuf_tensor("wo3_bf", [128, NCH * D], BF16)
    w1_bf = nc.alloc_sbuf_tensor("w1_bf", [128, NCH * 4096], BF16)
    w2t = nc.alloc_sbuf_tensor("w2t", [128, 4096], BF16)
    ctxbf = nc.alloc_sbuf_tensor("ctxbf", [128, NCH * 512], BF16)
    hstr = nc.alloc_sbuf_tensor("hstr", [128, 32 * 512], BF16)
    q3s = hstr[:, 0:NH * 512]
    x4s = nc.alloc_sbuf_tensor("x4s", [128, NCH * 512], F32)
    stage = nc.alloc_sbuf_tensor("stage", [128, 4224], F32)
    misc = nc.alloc_sbuf_tensor("misc", [128, 1024], F32)
    ones_rf = nc.alloc_sbuf_tensor("ones_rf", [1, 128], F32)
    ones_cf = nc.alloc_sbuf_tensor("ones_cf", [128, 1], F32)
    ones_cb = nc.alloc_sbuf_tensor("ones_cb", [128, 1], BF16)
    arow = nc.alloc_sbuf_tensor("arow", [1, 4096], F32)
    abc = nc.alloc_sbuf_tensor("abc", [128, 1024], F32)
    ps1 = nc.alloc_psum_tensor("ps1", [128, 512], F32)
    ps2 = nc.alloc_psum_tensor("ps2", [128, 512], F32)
    ps3 = nc.alloc_psum_tensor("ps3", [128, 512], F32)
    ps4 = nc.alloc_psum_tensor("ps4", [128, 512], F32)

    bo3_s = misc[:, 0:8]; fb1_s = misc[:, 8:40]; fb2_s = misc[:, 40:48]
    zrow = misc[:1, 48:560]
    bcz = misc[:, 560:1072] if False else stage[:, 3584:4096]

    p.memset("vector", ones_rf[:, :], 1.0)
    kvs = stage[:, 0:2 * NH * 65]
    p.dma("sync", kvs[0:65, :], kvp_in[:, 0:2 * NH * 65])
    p.dma("sync", stage[0:65, 2080:2080 + 2 * NH * 65], kvp_in[:, 2 * NH * 65:2 * 2 * NH * 65])
    p.nop("vector", waits=p.at("dsy"))
    for r in range(1, NCORES):
        if r > 1:
            p.dma("sync", stage[0:65, 2080:2080 + 2 * NH * 65], kvp_in[:, r * 2 * NH * 65:(r + 1) * 2 * NH * 65],
                  waits=[("v", p.n("v"))])
            p.nop("vector", waits=p.at("dsy"))
        p.tt("vector", kvs[0:65, :], kvs[0:65, :], stage[0:65, 2080:2080 + 2 * NH * 65], ALU.add,
             waits=p.at("dsy"))
    p.copy("vector", kvb[0:65, :], kvs[0:65, :])
    kv_v = p.at("v")
    for w_in, w_bf, gran in ((wo3T_in, wo3_bf, 4), (fw1_in, w1_bf, 1)):
        src3 = w_in.rearrange("(c p) e -> p c e", p=128)
        C, E = src3.shape[1], src3.shape[2]
        cpg = max(1, 4096 // E)
        for c0 in range(0, C, cpg):
            n = min(cpg, C - c0)
            p.dma("scalar", stage[:, 0:n * E].rearrange("p (c e) -> p c e", e=E),
                  src3[:, c0:c0 + n, :], waits=[("v", p.n("v"))])
            p.copy("vector", w_bf[:, c0 * E:(c0 + n) * E], stage[:, 0:n * E], waits=p.at("dsc"))
    p.dma("scalar", stage[:, 0:8], bo3_in, waits=[("v", p.n("v"))])
    p.dma("scalar", stage[:, 8:40], fb1_in)
    p.dma("scalar", stage[:, 40:48], fb2_in)
    p.copy("vector", misc[:, 0:48], stage[:, 0:48], waits=p.at("dsc"))
    p.memset("vector", ones_cf[:, :], 1.0)
    p.memset("vector", ones_cb[:, :], 1.0)
    wld = p.at("v")

    p.memset("vector", arow[:1, :], 0.0)
    for dk in range(NCH):
        p.act(w2t[:, 0:4096], w1_bf[:, dk * 4096:(dk + 1) * 4096], AF.Abs,
              waits=wld + [("v", p.n("v"))])
        aw = p.at("s")
        for seg in range(8):
            p.mm(ps1[0:1, :], ones_cb[:, :], w2t[:, seg * 512:(seg + 1) * 512], True, True,
                 waits=list(aw) + [("v", p.n("v"))], inc="t")
            p.tt("vector", arow[:1, seg * 512:(seg + 1) * 512], arow[:1, seg * 512:(seg + 1) * 512],
                 ps1[0:1, :], ALU.add, waits=p.at("t"))
    p.ts("vector", arow[:1, :], arow[:1, :], 1.0 / D, None, ALU.mult)
    for seg in range(8):
        p.mm(ps2[:, :], ones_rf[:, :], arow[:1, seg * 512:(seg + 1) * 512], True, True,
             waits=p.at("v"), inc="t")
        p.copy("vector", abc[:, 0:512], ps2[:, :], waits=p.at("t"))
        for dk in range(NCH):
            csl = w1_bf[:, dk * 4096 + seg * 512: dk * 4096 + (seg + 1) * 512]
            p.act(w2t[:, 0:512], csl, AF.Sign, waits=[("v", p.n("v")), ("t", p.n("t"))])
            p.tt("vector", csl, w2t[:, 0:512], abc[:, 0:512], ALU.mult, waits=p.at("s"))
    w1bin = p.at("v")

    w2src0 = fw2_in.rearrange("(c p) e -> p c e", p=128)
    p.memset("vector", arow[:1, 0:1024], 0.0)
    for hc in range(32):
        p.dma("scalar", stage[:, 0:1024], w2src0[:, hc, :], waits=[("s", p.n("s")), ("v", p.n("v"))])
        p.act(stage[:, 1024:2048], stage[:, 0:1024], AF.Abs, waits=p.at("dsc"))
        aw = p.at("s")
        for seg in range(2):
            p.mm(ps1[0:1, :], ones_cf[:, :], stage[:, 1024 + seg * 512:1024 + (seg + 1) * 512],
                 True, True, waits=list(aw) + [("v", p.n("v"))], inc="t")
            p.tt("vector", arow[:1, seg * 512:(seg + 1) * 512], arow[:1, seg * 512:(seg + 1) * 512],
                 ps1[0:1, :], ALU.add, waits=p.at("t"))
    p.ts("vector", arow[:1, 0:1024], arow[:1, 0:1024], 1.0 / 4096, None, ALU.mult)
    for seg in range(2):
        p.mm(ps2[:, :], ones_rf[:, :], arow[:1, seg * 512:(seg + 1) * 512], True, True,
             waits=p.at("v"), inc="t")
        p.copy("vector", abc[:, seg * 512:(seg + 1) * 512], ps2[:, :], waits=p.at("t"))
    a2v = p.at("v")
    for hc in range(32):
        p.dma("scalar", stage[:, 0:1024], w2src0[:, hc, :], waits=[("s", p.n("s")), ("v", p.n("v"))])
        p.act(w2t[:, 0:1024], stage[:, 0:1024], AF.Sign, waits=p.at("dsc"))
        p.tt("vector", w2t[:, 1024:2048], w2t[:, 0:1024], abc[:, 0:1024], ALU.mult,
             waits=a2v + [("s", p.n("s")), ("dgp", p.n("dgp"))])
        p.dma("gpsimd", bw2_scr[hc * 128:(hc + 1) * 128, :], w2t[:, 1024:2048],
              waits=[("v", p.n("v"))])
    w2bin = p.at("dgp", "v")

    for ts in range(NST):
        bsel = 0 if ts < 2 else 1
        p.dma("sync", q3s[0:64, :].rearrange("q (h t) -> q h t", t=512),
              q3_in.rearrange("q (h t) -> q h t", t=TC)[:, :, ts * 512:(ts + 1) * 512],
              waits=[("t", p.n("t"))])
        p.dma("gpsimd", x4s[:, :].rearrange("p (c t) -> p c t", t=512),
              x3_in[:, ts * 512:(ts + 1) * 512].rearrange("(c p) t -> p c t", p=128),
              waits=[("v", p.n("v"))])
        qld = p.at("dsy", "dgp")
        for h in range(NH):
            p.mm(ps1[0:65, :], kvb[0:64, (bsel * NH + h) * 65:(bsel * NH + h + 1) * 65],
                 q3s[0:64, h * 512:(h + 1) * 512], True, True,
                 waits=list(qld) + list(kv_v) + [("v", p.n("v")), ("s", p.n("s"))], inc="t")
            p.ts("vector", zrow, ps1[64:65, :], 1e-6, None, ALU.add, waits=p.at("t"))
            p.recip(zrow, zrow)
            p.mm(ps2[:, :], ones_rf[:, :], zrow, True, True, waits=p.at("v"), inc="t")
            p.copy("vector", bcz, ps2[:, :], waits=p.at("t"))
            p.tt("vector", ctxbf[(h % 2) * 64:(h % 2) * 64 + 64, (h // 2) * 512:(h // 2) * 512 + 512],
                 ps1[0:64, :], bcz[0:64, :], ALU.mult)
        ctx_v = p.at("v")
        for oc in range(NCH):
            for dk in range(NCH):
                p.mm(ps3[:, :], wo3_bf[:, dk * D + oc * 128: dk * D + (oc + 1) * 128],
                     ctxbf[:, dk * 512:(dk + 1) * 512], dk == 0, dk == NCH - 1,
                     waits=(list(wld) + list(ctx_v) + [("v", p.n("v"))]) if dk == 0 else (),
                     inc="t" if dk == NCH - 1 else None)
            xsl = x4s[:, oc * 512:(oc + 1) * 512]
            p.ts("vector", stage[:, 0:512], ps3[:, :], bo3_s[:, oc:oc + 1], None, ALU.add,
                 waits=p.at("t"))
            p.tt("vector", xsl, xsl, stage[:, 0:512], ALU.add)
        p.dma("scalar", ctxbf[:, :].rearrange("p (c t) -> p c t", t=512),
              xn3_in[:, ts * 512:(ts + 1) * 512].rearrange("(c p) t -> p c t", p=128),
              waits=[("t", p.n("t")), ("v", p.n("v"))])
        xnld = p.at("dsc")
        for oc in range(32):
            psx = [ps1, ps2][oc % 2]
            for dk in range(NCH):
                p.mm(psx[:, :], w1_bf[:, dk * 4096 + oc * 128: dk * 4096 + (oc + 1) * 128],
                     ctxbf[:, dk * 512:(dk + 1) * 512], dk == 0, dk == NCH - 1,
                     waits=(list(xnld) + list(w1bin) + [("s", p.n("s")), ("v", p.n("v"))]) if dk == 0 else (),
                     inc="t" if dk == NCH - 1 else None)
            p.act(hstr[:, oc * 512:(oc + 1) * 512], psx[:, :], AF.Gelu,
                  bias=fb1_s[:, oc:oc + 1], waits=p.at("t"))
        h_s = p.at("s")
        w2src = bw2_scr.rearrange("(c p) e -> p c e", p=128)
        for oc in range(NCH):
            psx = [ps3, ps4][oc % 2]
            for p0 in range(0, 32, 16):
                slot = (p0 // 16) % 2
                p.dma("scalar", w2t[:, slot * 2048: slot * 2048 + 2048]
                          .rearrange("p (c e) -> p c e", e=128),
                      w2src[:, p0:p0 + 16, oc * 128:(oc + 1) * 128],
                      waits=[("t", p.n("t"))] + (list(w2bin) if (ts == 0 and oc == 0) else []))
                wv2 = p.at("dsc")
                for dk in range(p0, p0 + 16):
                    p.mm(psx[:, :], w2t[:, slot * 2048 + (dk - p0) * 128:
                                        slot * 2048 + (dk - p0) * 128 + 128],
                         hstr[:, dk * 512:(dk + 1) * 512], dk == 0, dk == 31,
                         waits=(list(h_s) + list(wv2) + [("v", p.n("v"))]) if dk == p0 else (),
                         inc="t" if dk == 31 else None)
            xsl = x4s[:, oc * 512:(oc + 1) * 512]
            p.ts("vector", stage[:, 512:1024], psx[:, :], fb2_s[:, oc:oc + 1], None, ALU.add,
                 waits=p.at("t"))
            p.tt("vector", xsl, xsl, stage[:, 512:1024], ALU.add)
        p.dma("gpsimd", out_out.rearrange("(c p) t -> p c t", p=128)[:, :, ts * 512:(ts + 1) * 512],
              x4s[:, :].rearrange("p (c t) -> p c t", t=512), waits=p.at("v"))
    p.nop("gpsimd", waits=p.at("dgp"))
    p.nop("sync", waits=p.at("dgp", "dsy", "dsc"))
    return p


def host_H(inputs, x3_sh, xn3_sh, q3_sh, kvp_sh):
    kvp_all = np.concatenate(kvp_sh, axis=1)
    wo3T = fmT(inputs["L3_attn_Wo"]).astype(np.float32)
    bo3 = np.ascontiguousarray(np.asarray(inputs["L3_attn_bo"]).reshape(8, 128).T).astype(np.float32)
    fw1 = fmT(inputs["L3_ffn_w1"]).astype(np.float32)
    fb1 = np.ascontiguousarray(np.asarray(inputs["L3_ffn_b1"]).reshape(32, 128).T).astype(np.float32)
    fw2 = fmT(inputs["L3_ffn_w2"]).astype(np.float32)
    fb2 = np.ascontiguousarray(np.asarray(inputs["L3_ffn_b2"]).reshape(8, 128).T).astype(np.float32)
    return [{"x3": x3_sh[core], "xn3": xn3_sh[core], "q3": q3_sh[core], "kvp": kvp_all,
             "wo3T": wo3T, "bo3": bo3, "fw1": fw1, "fb1": fb1, "fw2": fw2, "fb2": fb2}
            for core in range(NCORES)]


_BUILT = {}


def _get(name):
    if name not in _BUILT:
        _BUILT[name] = globals()["build_" + name]().finalize()
    return _BUILT[name]


def kernel(**inputs):
    cores = list(range(NCORES))
    x = np.asarray(inputs["L0_attn_x"])
    xsh_fm = [fmT(s).astype(np.float32) for s in shard_rows(x)]

    ra = run_bass_kernel_spmd(_get("A"), host_A(inputs, xsh_fm), core_ids=cores).results
    xn0_sh = [r["xn0"] for r in ra]
    ckv_sh = [r["ckv"] for r in ra]
    mw_sh = [r["mw"] for r in ra]

    rb = run_bass_kernel_spmd(_get("B"), host_B(inputs, xsh_fm, xn0_sh, ckv_sh, mw_sh),
                              core_ids=cores).results
    x1a_sh = [r["x1a"] for r in rb]
    ye_all = [r["ye"] for r in rb]

    rc = run_bass_kernel_spmd(_get("C"), host_C(inputs, x1a_sh, ye_all), core_ids=cores).results
    del rb, ye_all, x1a_sh
    x1_sh = [r["x1"] for r in rc]
    xn1_sh = [r["xn1"] for r in rc]
    q1_sh = [r["q1"] for r in rc]
    k1_sh = [r["k1"] for r in rc]
    v1_sh = [r["vext"] for r in rc]
    gmp_sh = [r["gmp"] for r in rc]

    rd = run_bass_kernel_spmd(_get("D"), host_D(inputs, x1_sh, xn1_sh, q1_sh, k1_sh, v1_sh, gmp_sh),
                              core_ids=cores).results
    del rc, x1_sh, xn1_sh, q1_sh, k1_sh, v1_sh, gmp_sh
    x2_sh = [r["x2"] for r in rd]
    xn2_sh = [r["xn2"] for r in rd]
    q2_sh = [r["q2"] for r in rd]
    k2_sh = [r["k2"] for r in rd]
    v2_sh = [r["v2e"] for r in rd]

    re_ = run_bass_kernel_spmd(_get("E"), host_E(inputs, q2_sh, k2_sh, v2_sh), core_ids=cores).results
    ctxp_sh = [r["ctxp"] for r in re_]
    rowc_sh = [r["rowc"] for r in re_]

    rf = run_bass_kernel_spmd(_get("F"), host_F(inputs, x2_sh, xn2_sh, ctxp_sh, rowc_sh),
                              core_ids=cores).results
    del rd, re_, x2_sh, xn2_sh, q2_sh, k2_sh, v2_sh, ctxp_sh, rowc_sh
    x3_sh = [r["x3"] for r in rf]

    rg = run_bass_kernel_spmd(_get("G"), host_G(inputs, x3_sh), core_ids=cores).results
    xn3_sh = [r["xn3"] for r in rg]
    q3_sh = [r["q3"] for r in rg]
    kvp_sh = [r["kvp"] for r in rg]

    rh = run_bass_kernel_spmd(_get("H"), host_H(inputs, x3_sh, xn3_sh, q3_sh, kvp_sh),
                              core_ids=cores).results
    out_sh = [np.ascontiguousarray(r["out"].T) for r in rh]
    return unshard_rows(out_sh)
